# version 14
# baseline (speedup 1.0000x reference)
import numpy as np
from contextlib import ExitStack

import concourse.bass as bass
import concourse.mybir as mybir
from concourse.bass_utils import run_bass_kernel_spmd

F32 = mybir.dt.float32
I32 = mybir.dt.int32
AF = mybir.ActivationFunctionType
ALU = mybir.AluOpType
AX = mybir.AxisListType

TOK = 2048
NCH = 16
D = 1024
FCN = 8
NH = 16
NWIN = 4
NEXP = 8
FF = 4096
NFC = 32
TTK = 256
NTT = TOK // TTK
C0 = 20.0
EPS = 1e-5

ENGS = ("sync", "gpsimd", "vector", "scalar", "tensor")


class Em:

    def __init__(self, sems, dma_ch):
        self.sems = sems
        self.dma_ch = dma_ch
        self.cnt = {k: 0 for k in sems}
        self.q = {e: [] for e in ENGS}
        self.last = {}

    def op(self, eng, f, inc=None, n=1):
        if inc is not None:
            self.cnt[inc] += n
            v = self.cnt[inc]
            h = self.sems[inc]

            def g(E, f=f, h=h, n=n):
                f(E).then_inc(h, n)

            self.q[eng].append(g)
            return v
        self.q[eng].append(lambda E, f=f: f(E))
        return None

    def wait(self, eng, sem, val):
        if val is None or val <= 0:
            return
        h = self.sems[sem]
        self.q[eng].append(lambda E, h=h, v=int(val): E.wait_ge(h, v))

    def consumed(self, key, sem, val):
        d = self.last.setdefault(key, {})
        d[sem] = max(d.get(sem, 0), val)

    def guardw(self, eng, key):
        d = self.last.pop(key, None)
        if d:
            for s, v in d.items():
                self.wait(eng, s, v)

    def flush(self, nc, defer=()):
        for ch, eng in self.dma_ch.items():
            if self.cnt[ch] and ch not in defer:
                self.wait(eng, ch, self.cnt[ch])
        with nc.Block() as blk:
            for e in ENGS:
                if self.q[e]:
                    getattr(blk, e)(lambda E, lst=tuple(self.q[e]): [g(E) for g in lst])
        self.q = {e: [] for e in ENGS}
        self.last = {}


def build(debug_spills=False, bisect=None):
    nc = bass.Bass("TRN2", target_bir_lowering=False, detect_race_conditions=False)

    x_d = nc.dram_tensor("x", [TOK, D], F32, kind="ExternalInput").ap()
    x0_d = nc.dram_tensor("x0", [256, D], F32, kind="ExternalInput").ap()
    pos_d = nc.dram_tensor("pos", [TOK, D], F32, kind="ExternalInput").ap()
    pos0_d = nc.dram_tensor("pos0", [256, D], F32, kind="ExternalInput").ap()
    gmT_d = nc.dram_tensor("gmT", [D, 64], F32, kind="ExternalInput").ap()
    cwl_d = nc.dram_tensor("cwl", [D, 4, D], F32, kind="ExternalInput").ap()
    gw_d = nc.dram_tensor("gw", [2 * D, D], F32, kind="ExternalInput").ap()
    gateb_d = nc.dram_tensor("gateb", [128, 8], F32, kind="ExternalInput").ap()
    ow_d = nc.dram_tensor("ow", [D, D], F32, kind="ExternalInput").ap()
    outb_d = nc.dram_tensor("outb", [128, D], F32, kind="ExternalInput").ap()
    wgv_d = nc.dram_tensor("wgv", [128, 64], F32, kind="ExternalInput").ap()
    alpha_d = nc.dram_tensor("alpha", [128, 8], F32, kind="ExternalInput").ap()
    lnw_d = nc.dram_tensor("lnw", [128, 64], F32, kind="ExternalInput").ap()
    lnb_d = nc.dram_tensor("lnb", [128, 64], F32, kind="ExternalInput").ap()
    w1l_d = nc.dram_tensor("w1l", [NEXP * 16 * 128, 2048], F32, kind="ExternalInput").ap()
    w2_d = nc.dram_tensor("w2", [NEXP * 16 * 128, 2048], F32, kind="ExternalInput").ap()
    f1b_d = nc.dram_tensor("f1b", [128, NEXP * 32], F32, kind="ExternalInput").ap()
    f2b_d = nc.dram_tensor("f2b", [NEXP, D], F32, kind="ExternalInput").ap()
    convb_d = nc.dram_tensor("convb", [128, 8], F32, kind="ExternalInput").ap()
    ident_d = nc.dram_tensor("ident", [128, 128], F32, kind="ExternalInput").ap()
    out_d = nc.dram_tensor("out", [TOK, D], F32, kind="ExternalOutput").ap()

    spill_kind = "ExternalOutput" if debug_spills else "Internal"
    lb_tok_sp = nc.dram_tensor("lb_tok_sp", [TOK, D], F32, kind=spill_kind).ap()
    s_sp = nc.dram_tensor("s_sp", [TOK, D], F32, kind=spill_kind).ap()
    mix_sp = nc.dram_tensor("mix_sp", [128, NCH * 1024], F32, kind="Internal").ap()
    y_dram = nc.dram_tensor("y_dram", [24 * 256 + 1, D], F32,
                            kind="ExternalOutput" if debug_spills else "Internal").ap()
    gl2_dram = nc.dram_tensor("gl2_dram", [24 * 256 + 1, 2], F32, kind="Internal").ap()
    iot_d = nc.dram_tensor("iot", [128, 2048], F32, kind="ExternalInput").ap()
    tok_d = nc.dram_tensor("tokt", [128, 16], F32, kind="ExternalInput").ap()
    pio_d = nc.dram_tensor("pio", [128, 1], F32, kind="ExternalInput").ap()
    btab_d = nc.dram_tensor("btab", [NEXP * 128, 48], F32, kind="ExternalInput").ap()
    if debug_spills:
        dbg_local = nc.dram_tensor("dbg_local", [128, 16384], F32, kind="ExternalOutput").ap()
        dbg_glob = nc.dram_tensor("dbg_glob", [128, 16384], F32, kind="ExternalOutput").ap()
        dbg_mixed = nc.dram_tensor("dbg_mixed", [128, 16384], F32, kind="ExternalOutput").ap()
        dbg_rt = nc.dram_tensor("dbg_rt", [128, 560], F32, kind="ExternalOutput").ap()

    with ExitStack() as ctx:
        def sbuf(name, rows, cols):
            return nc.alloc_sbuf_tensor(name, [rows, cols], F32).ap()

        RA = sbuf("RA", 128, 16384)
        RB = sbuf("RB", 128, 16384)
        RC = sbuf("RC", 128, 11264)
        ident = sbuf("ident_s", 128, 128)
        ones128 = sbuf("ones128", 128, 128)
        outb_sb = sbuf("outb_sb", 128, D)
        alpha_sb = sbuf("alpha_sb", 128, 8)
        wgsum_sb = sbuf("wgsum_sb", 128, 8)
        convb_sb = sbuf("convb_sb", 128, 8)
        gateb_sb = sbuf("gateb_sb", 128, 8)
        lnw_sb = sbuf("lnw_sb", 128, 64)
        lnb_sb = sbuf("lnb_sb", 128, 64)
        f1b_sb = sbuf("f1b_sb", 128, NEXP * 32)
        f2b_sb = sbuf("f2b_sb", 8, D)
        wg_sb = sbuf("wg_sb", 128, 64)
        combT_sb = sbuf("combT_sb", 8, TOK)
        comb_sb = sbuf("comb_sb", 128, NCH * 8)
        gtT_sb = sbuf("gtT_sb", 128, D)
        gtx_sb = sbuf("gtx_sb", 128, NH * 65)
        mu2_sb = sbuf("mu2_sb", 128, NCH)
        std2_sb = sbuf("std2_sb", 128, NCH)
        d12_sb = sbuf("d12_sb", 128, NCH)
        g1_sb = sbuf("g1_sb", 128, NCH)
        g2_sb = sbuf("g2_sb", 128, NCH)
        msk1_sb = sbuf("msk1_sb", 128, NCH * 8)
        msk2_sb = sbuf("msk2_sb", 128, NCH * 8)
        eps_t = sbuf("eps_t", 128, 1)
        c20n = sbuf("c20n", 128, 1)
        scr = sbuf("scr", 128, 96)
        lgA = sbuf("lgA", 128, 64)
        lgB = sbuf("lgB", 128, 32)
        rt_sb = sbuf("rt_sb", 128, 560)

        ps = nc.alloc_psum_tensor("ps", [128, 4096], F32).ap()

        def bank(b, n=512, off=0):
            return ps[:, b * 512 + off: b * 512 + off + n]

        dma_ch = {"cPRE": "sync", "cSP0": "sync", "cSP1": "sync", "cGW": "sync",
                  "cW10": "sync", "cW11": "sync", "cBT0": "sync", "cBT1": "sync",
                  "cX0": "gpsimd", "cX1": "gpsimd", "cSS0": "gpsimd", "cSS1": "gpsimd",
                  "cMX0": "gpsimd", "cMX1": "gpsimd", "cW20": "gpsimd", "cW21": "gpsimd",
                  "cGA0": "gpsimd", "cGA1": "gpsimd"}
        sems = {}
        for name in list(dma_ch) + ["tM", "sM", "vM"]:
            sems[name] = ctx.enter_context(nc.semaphore(name))
        em = Em(sems, dma_ch)
        mk = {}

        SC_SUM, SC_MNEG, SC_SQS, SC_V, SC_STD0, SC_R0, SC_Q, SC_S1, SC_STD1, SC_RSTD = range(10)
        SC_M1, SC_M2, SC_RL, SC_T1 = 10, 11, 12, 13

        def sc(i, g=0):
            j = g * 10 + i
            return scr[:, j:j + 1]

        def rc(a, b):
            return RC[:, a:b]

        def emit_dummy_out():
            DOF = [rc(i * 1024, (i + 1) * 1024) for i in range(2)]
            for c in range(NCH):
                par = c % 2
                em.guardw("gpsimd", ("dxf", par))
                xl = em.op("gpsimd", lambda E, c=c, par=par:
                           E.dma_start(DOF[par], x_d[c * 128:(c + 1) * 128, :]),
                           inc="cX%d" % par, n=16)
                em.wait("sync", "cX%d" % par, xl)
                om = em.op("sync", lambda E, c=c, par=par:
                           E.dma_start(out_d[c * 128:(c + 1) * 128, :], DOF[par]),
                           inc="cSP%d" % par, n=16)
                em.consumed(("dxf", par), "cSP%d" % par, om)
            em.flush(nc)

        XSL = [rc(i * 1024, (i + 1) * 1024) for i in range(4)]
        LBS = [rc(4096 + i * 1024, 4096 + (i + 1) * 1024) for i in range(4)]
        SQS = rc(8192, 9216)
        LBT = [rc(9216 + i * 1024, 9216 + (i + 1) * 1024) for i in range(2)]
        XA0 = [LBS[0], LBS[1]]

        em.op("vector", lambda E: E.memset(eps_t, EPS))
        em.op("vector", lambda E: E.memset(c20n, -C0))
        mk["consts"] = em.op("vector", lambda E: E.memset(ones128[:, :], 1.0), inc="vM")

        for nm, dst, src in [
            ("ident", ident[:, :], ident_d),
            ("wg", wg_sb[:, :], wgv_d),
            ("alpha", alpha_sb[:, :], alpha_d),
            ("gateb", gateb_sb[:, :], gateb_d),
            ("convb", convb_sb[:, :], convb_d),
            ("lnw", lnw_sb[:, :], lnw_d),
            ("lnb", lnb_sb[:, :], lnb_d),
            ("f1b", f1b_sb[:, :], f1b_d),
            ("f2b", f2b_sb[:, :], f2b_d),
            ("outb", outb_sb[:, :], outb_d),
            ("gm", gtT_sb.rearrange("p (f g) -> p f g", g=128)[:, :, 64:128],
             gmT_d.rearrange("(f p) g -> p f g", p=128)),
            ("iot", rt_sb[0:8, 529:553], iot_d[0:8, 0:24]),
            ("tokt", rt_sb[:, 512:528], tok_d),
            ("pio", rt_sb[0:128, 528:529], pio_d),
        ]:
            mk[("pre", nm)] = em.op("sync", lambda E, d=dst, s=src: E.dma_start(d, s), inc="cPRE", n=16)
        PRE_ALL = mk[("pre", "gm")]

        em.wait("tensor", "cPRE", PRE_ALL)
        em.wait("tensor", "vM", mk["consts"])
        for kc in range(8):
            last = kc == 7
            v = em.op("tensor", lambda E, kc=kc:
                      E.matmul(bank(5, 8), ones128[:, :], wg_sb[:, kc * 8:(kc + 1) * 8],
                               start=(kc == 0), stop=(kc == 7)), inc="tM" if last else None)
        mk["wgsmm"] = v
        em.wait("vector", "tM", mk["wgsmm"])
        mk["wgs"] = em.op("vector", lambda E: E.tensor_copy(wgsum_sb[:, :], bank(5, 8)), inc="vM")
        em.consumed(("bank", 5), "vM", mk["wgs"])

        def ln_batch(jobs):
            def step(eng, fns, inc):
                m = None
                for f in fns:
                    m = em.op(eng, f, inc=inc)
                return m
            m = step("vector", [lambda E, s=s, g=g: E.reduce_sum(sc(SC_SUM, g), s, axis=AX.X)
                                for s, _, g, _, _ in jobs], "vM")
            em.wait("scalar", "vM", m)
            fns = []
            for s, _, g, _, mu in jobs:
                fns.append(lambda E, g=g: E.mul(sc(SC_MNEG, g), sc(SC_SUM, g), -1.0 / D))
                if mu is not None:
                    fns.append(lambda E, g=g, mu=mu: E.mul(mu, sc(SC_SUM, g), 1.0 / D))
            m = step("scalar", fns, "sM")
            em.wait("scalar", "sM", m)
            m = step("scalar", [lambda E, s=s, g=g:
                                E.activation(SQS, s, AF.Square, bias=sc(SC_MNEG, g), scale=1.0,
                                             accum_out=sc(SC_SQS, g)) for s, _, g, _, _ in jobs], "sM")
            sq_m = m
            em.wait("vector", "sM", m)
            m = step("vector", [lambda E, g=g: E.tensor_scalar(sc(SC_V, g), sc(SC_SQS, g), 1.0 / D,
                                                               EPS, ALU.mult, ALU.add)
                                for _, _, g, _, _ in jobs], "vM")
            em.wait("scalar", "vM", m)
            m = step("scalar", [lambda E, g=g: E.activation(sc(SC_STD0, g), sc(SC_V, g), AF.Sqrt)
                                for _, _, g, _, _ in jobs], "sM")
            em.wait("vector", "sM", m)
            m = step("vector", [lambda E, g=g: E.reciprocal(sc(SC_R0, g), sc(SC_STD0, g))
                                for _, _, g, _, _ in jobs], "vM")
            em.wait("vector", "vM", m)
            m = step("vector", [lambda E, g=g: E.tensor_mul(sc(SC_Q, g), sc(SC_V, g), sc(SC_R0, g))
                                for _, _, g, _, _ in jobs], "vM")
            em.wait("vector", "vM", m)
            m = step("vector", [lambda E, g=g: E.tensor_add(sc(SC_S1, g), sc(SC_STD0, g), sc(SC_Q, g))
                                for _, _, g, _, _ in jobs], "vM")
            em.wait("vector", "vM", m)
            sds = [sd if sd is not None else sc(SC_STD1, g) for _, _, g, sd, _ in jobs]
            m = step("vector", [lambda E, g=g, d=d: E.tensor_scalar_mul(d, sc(SC_S1, g), 0.5)
                                for (_, _, g, _, _), d in zip(jobs, sds)], "vM")
            em.wait("vector", "vM", m)
            m = step("vector", [lambda E, g=g, s=d: E.reciprocal(sc(SC_RSTD, g), s)
                                for (_, _, g, _, _), d in zip(jobs, sds)], "vM")
            em.wait("vector", "vM", m)
            return [em.op("vector", lambda E, s=s, d=d, g=g:
                          E.tensor_scalar(d, s, sc(SC_MNEG, g), sc(SC_RSTD, g), ALU.add, ALU.mult),
                          inc="vM") for s, d, g, _, _ in jobs], sq_m

        def scM(k):
            return scr[:, 80 + k: 81 + k]

        def emit_top2(batch):
            cs = [c for c in batch]

            def la(c):
                return lgA[:, ((c // 4) % 2) * 32 + (c % 4) * 8: ((c // 4) % 2) * 32 + (c % 4) * 8 + 8]

            def lb_(c):
                return lgB[:, (c % 4) * 8: (c % 4) * 8 + 8]
            m = None
            for c in cs:
                m = em.op("vector", lambda E, c=c: E.tensor_scalar(lb_(c), wgsum_sb[:, :],
                                                                   mu2_sb[:, c:c + 1], None,
                                                                   ALU.mult), inc="vM")
            em.wait("vector", "vM", m)
            for c in cs:
                m = em.op("vector", lambda E, c=c: E.tensor_add(la(c), la(c), lb_(c)), inc="vM")
            em.wait("vector", "vM", m)
            for c in cs:
                m = em.op("vector", lambda E, c=c: E.reduce_max(scM(c % 4), la(c), axis=AX.X), inc="vM")
            em.wait("vector", "vM", m)
            for c in cs:
                m = em.op("vector", lambda E, c=c: E.tensor_scalar(msk1_sb[:, c * 8:(c + 1) * 8],
                                                                   la(c), scM(c % 4), None,
                                                                   ALU.is_equal), inc="vM")
            em.wait("vector", "vM", m)
            for c in cs:
                m = em.op("vector", lambda E, c=c:
                          E.scalar_tensor_tensor(lb_(c), msk1_sb[:, c * 8:(c + 1) * 8], -1e30,
                                                 la(c), ALU.mult, ALU.add), inc="vM")
            em.wait("vector", "vM", m)
            for c in cs:
                m = em.op("vector", lambda E, c=c: E.reduce_max(scM(4 + c % 4), lb_(c), axis=AX.X),
                          inc="vM")
            em.wait("vector", "vM", m)
            for c in cs:
                m = em.op("vector", lambda E, c=c: E.tensor_scalar(msk2_sb[:, c * 8:(c + 1) * 8],
                                                                   lb_(c), scM(4 + c % 4), None,
                                                                   ALU.is_equal), inc="vM")
            em.wait("vector", "vM", m)
            for c in cs:
                m = em.op("vector", lambda E, c=c: E.tensor_tensor(d12_sb[:, c:c + 1], scM(c % 4),
                                                                   scM(4 + c % 4), ALU.subtract),
                          inc="vM")
                mk[("d12", c)] = m
            em.wait("vector", "vM", m)

        batches = [list(range(b * 4, min(b * 4 + 4, 18))) for b in range(5)]
        ti = 0
        for c in batches[0]:
            em.guardw("gpsimd", ("xsl", c % 4))
            mk[("x", c)] = em.op("gpsimd", lambda E, d=XSL[c % 4], s=x_d[c * 128:(c + 1) * 128, :]:
                                 E.dma_start(d, s), inc="cX%d" % (c % 2), n=16)
        for b, batch in enumerate(batches):
            em.wait("vector", "cX%d" % (batch[-1] % 2), mk[("x", batch[-1])])
            if len(batch) > 1:
                em.wait("vector", "cX%d" % (batch[-2] % 2), mk[("x", batch[-2])])
            if b == 4:
                for q in (0, 1):
                    em.guardw("vector", ("lbs", q))
            jobs = [(XSL[c % 4],
                     RA[:, c * 1024:(c + 1) * 1024] if c < 16 else XA0[c - 16],
                     c % 4, None, None) for c in batch]
            vms, sq1 = ln_batch(jobs)
            for c, v in zip(batch, vms):
                em.consumed(("xsl", c % 4), "vM", v)
                mk[("xn", c)] = v
            for c in batch:
                em.guardw("gpsimd", ("xsl", c % 4))
                psrc = pos_d[c * 128:(c + 1) * 128, :] if c < 16 else \
                    pos0_d[(c - 16) * 128:(c - 15) * 128, :]
                mk[("pl", c)] = em.op("gpsimd", lambda E, d=XSL[c % 4], s=psrc: E.dma_start(d, s),
                                      inc="cSS%d" % (c % 2), n=16)
            if b < 4:
                for c in batch:
                    em.guardw("vector", ("lbs", c % 4))
                jobs = [(RA[:, c * 1024:(c + 1) * 1024], LBS[c % 4], 4 + c % 4,
                         std2_sb[:, c:c + 1], mu2_sb[:, c:c + 1]) for c in batch]
                lbvms, sq2 = ln_batch(jobs)
                for c, v in zip(batch, lbvms):
                    mk[("lb", c)] = v
                em.wait("vector", "sM", sq2)
            else:
                em.wait("vector", "sM", sq1)
            for c in batch:
                em.wait("vector", "cSS%d" % (c % 2), mk[("pl", c)])
                dst = RA[:, c * 1024:(c + 1) * 1024] if c < 16 else XA0[c - 16]
                mk[("padd", c)] = em.op("vector", lambda E, d=dst, p=XSL[c % 4]:
                                        E.tensor_add(d, d, p), inc="vM")
                em.consumed(("xsl", c % 4), "vM", mk[("padd", c)])
            if b + 1 < len(batches):
                for c in batches[b + 1]:
                    em.guardw("gpsimd", ("xsl", c % 4))
                    src = x_d[c * 128:(c + 1) * 128, :] if c < 16 else \
                        x0_d[(c - 16) * 128:(c - 15) * 128, :]
                    mk[("x", c)] = em.op("gpsimd", lambda E, d=XSL[c % 4], s=src: E.dma_start(d, s),
                                         inc="cX%d" % (c % 2), n=16)
            if b == 4:
                break

            for c in batch:
                em.wait("sync", "vM", mk[("lb", c)])
                spmk = em.op("sync", lambda E, c=c: E.dma_start(lb_tok_sp[c * 128:(c + 1) * 128, :],
                                                                LBS[c % 4]),
                             inc="cSP%d" % (c % 2), n=16)
                em.consumed(("lbs", c % 4), "cSP%d" % (c % 2), spmk)

            for c in batch:
                em.wait("tensor", "vM", mk[("lb", c)])
                for fc in range(FCN):
                    em.guardw("tensor", ("bank", fc % 2))
                    if fc == 0:
                        em.guardw("tensor", ("lbt", c % 2))
                    tmk = em.op("tensor", lambda E, c=c, fc=fc:
                                E.transpose(bank(fc % 2, 128), LBS[c % 4][:, fc * 128:(fc + 1) * 128],
                                            ident[:, :]), inc="tM")
                    em.wait("vector", "tM", tmk)
                    emk = em.op("vector", lambda E, c=c, fc=fc:
                                E.tensor_copy(LBT[c % 2][:, fc * 128:(fc + 1) * 128], bank(fc % 2, 128)),
                                inc="vM")
                    em.consumed(("bank", fc % 2), "vM", emk)
                em.consumed(("lbs", c % 4), "tM", tmk)
                em.wait("tensor", "vM", emk)
                em.guardw("tensor", ("bank", 4 + c % 4))
                for fc in range(FCN):
                    last = fc == 7
                    v = em.op("tensor", lambda E, c=c, fc=fc:
                              E.matmul(bank(4 + c % 4, 8), LBT[c % 2][:, fc * 128:(fc + 1) * 128],
                                       wg_sb[:, fc * 8:(fc + 1) * 8], start=(fc == 0), stop=(fc == 7)),
                              inc="tM" if last else None)
                mk[("lgt", c)] = v
                em.consumed(("lbt", c % 2), "tM", v)
            em.wait("vector", "vM", mk["wgs"])
            m = None
            for c in batch:
                em.wait("vector", "tM", mk[("lgt", c)])
                m = em.op("vector", lambda E, c=c:
                          E.tensor_scalar(lgA[:, ((c // 4) % 2) * 32 + (c % 4) * 8:
                                              ((c // 4) % 2) * 32 + (c % 4) * 8 + 8],
                                          bank(4 + c % 4, 8), std2_sb[:, c:c + 1], None,
                                          ALU.mult), inc="vM")
                em.consumed(("bank", 4 + c % 4), "vM", m)
            em.wait("vector", "vM", m)
            for c in batch:
                for fc in range(FCN):
                    em.wait("tensor", "vM", mk[("padd", c)])
                    em.guardw("tensor", ("bank", 2 + ti % 2))
                    tmk = em.op("tensor", lambda E, c=c, fc=fc, b_=2 + ti % 2:
                                E.transpose(bank(b_, 128),
                                            RA[:, c * 1024 + fc * 128: c * 1024 + (fc + 1) * 128],
                                            ident[:, :]), inc="tM")
                    em.wait("vector", "tM", tmk)
                    emk = em.op("vector", lambda E, c=c, fc=fc, b_=2 + ti % 2:
                                E.tensor_copy(RB[:, fc * 2048 + c * 128: fc * 2048 + (c + 1) * 128],
                                              bank(b_, 128)), inc="vM")
                    em.consumed(("bank", 2 + ti % 2), "vM", emk)
                    ti += 1
            if b >= 1:
                emit_top2(batches[b - 1])
        emit_top2(batches[3])

        XA0T = rc(0, 2048)
        GTOK = rc(2048, 3072)
        CWR = [LBS[2], LBS[3], SQS, XSL[3]]
        em.guardw("tensor", ("xsl", 0))
        em.guardw("tensor", ("xsl", 1))
        for k in range(2):
            for fc in range(FCN):
                em.wait("tensor", "vM", mk[("padd", 16 + k)])
                em.guardw("tensor", ("bank", 2 + ti % 2))
                tmk = em.op("tensor", lambda E, k=k, fc=fc, b_=2 + ti % 2:
                            E.transpose(bank(b_, 128), XA0[k][:, fc * 128:(fc + 1) * 128], ident[:, :]),
                            inc="tM")
                em.wait("vector", "tM", tmk)
                emk = em.op("vector", lambda E, k=k, fc=fc, b_=2 + ti % 2:
                            E.tensor_copy(XA0T[:, fc * 256 + k * 128: fc * 256 + (k + 1) * 128],
                                          bank(b_, 128)), inc="vM")
                em.consumed(("bank", 2 + ti % 2), "vM", emk)
                em.consumed(("xsl", 0), "vM", emk)
                em.consumed(("xsl", 1), "vM", emk)
                ti += 1
        MKXA0T = emk

        em.wait("tensor", "vM", MKXA0T)
        for q in (2, 3):
            em.guardw("sync", ("lbs", q))
        em.guardw("sync", ("xsl", 3))
        em.wait("sync", "sM", em.cnt["sM"])
        flat = 0
        for p_ in range(2):
            for tidx, (k, ic) in enumerate([(k, ic) for k in range(4) for ic in range(8)]):
                em.guardw("sync", ("cwr", flat % 4))
                src = cwl_d[ic * 128:(ic + 1) * 128, k, :]
                cmk = em.op("sync", lambda E, d=CWR[flat % 4], s=src: E.dma_start(d, s),
                            inc=["cW10", "cW11", "cBT0", "cBT1"][flat % 4], n=16)
                em.wait("tensor", ["cW10", "cW11", "cBT0", "cBT1"][flat % 4], cmk)
                if tidx == 0:
                    for osl in range(4):
                        em.guardw("tensor", ("bank", 4 + osl))
                rhs = XA0T.rearrange("p (f t) -> p f t", t=256)[:, ic, :][:, k:256:4]
                for osl in range(4):
                    og = p_ * 4 + osl
                    lastmm = osl == 3
                    v = em.op("tensor", lambda E, flat=flat, osl=osl, og=og, tidx=tidx, rhs=rhs:
                              E.matmul(bank(4 + osl, 64), CWR[flat % 4][:, og * 128:(og + 1) * 128],
                                       rhs, start=(tidx == 0), stop=(tidx == 31)),
                              inc="tM" if lastmm else None)
                em.consumed(("cwr", flat % 4), "tM", v)
                flat += 1
            for osl in range(4):
                og = p_ * 4 + osl
                em.wait("vector", "tM", v)
                em.wait("vector", "cPRE", PRE_ALL)
                gmk = em.op("vector", lambda E, osl=osl, og=og:
                            E.tensor_scalar(gtT_sb[:, og * 128: og * 128 + 64], bank(4 + osl, 64),
                                            convb_sb[:, og:og + 1], None, ALU.add), inc="vM")
                em.consumed(("bank", 4 + osl), "vM", gmk)
        MKCONV = gmk

        for fc in range(FCN):
            em.wait("tensor", "vM", MKCONV)
            em.guardw("tensor", ("bank", 4 + fc % 2))
            tmk = em.op("tensor", lambda E, fc=fc:
                        E.transpose(bank(4 + fc % 2, 128), gtT_sb[:, fc * 128:(fc + 1) * 128],
                                    ident[:, :]), inc="tM")
            em.wait("vector", "tM", tmk)
            emk = em.op("vector", lambda E, fc=fc:
                        E.tensor_copy(GTOK[:, fc * 128:(fc + 1) * 128], bank(4 + fc % 2, 128)),
                        inc="vM")
            em.consumed(("bank", 4 + fc % 2), "vM", emk)
        em.op("vector", lambda E: E.memset(gtx_sb[:, 64::65], 1.0))
        for h in range(NH):
            gxmk = em.op("vector", lambda E, h=h:
                         E.tensor_copy(gtx_sb[:, h * 65: h * 65 + 64], GTOK[:, h * 64:(h + 1) * 64]),
                         inc="vM" if h == NH - 1 else None)
        MKGTX = gxmk

        em.wait("scalar", "vM", mk[("d12", 15)])
        em.op("scalar", lambda E: E.activation(g1_sb[:, :], d12_sb[:, :], AF.Sigmoid))
        mk["sig2"] = em.op("scalar", lambda E: E.activation(g2_sb[:, :], d12_sb[:, :], AF.Sigmoid,
                                                            scale=-1.0), inc="sM")
        em.wait("vector", "sM", mk["sig2"])
        for c4 in range(0, NCH, 4):
            grp = range(c4, c4 + 4)
            m = None
            for c in grp:
                m = em.op("vector", lambda E, c=c:
                          E.tensor_scalar(lgB[:, (c % 4) * 8:(c % 4) * 8 + 8],
                                          msk1_sb[:, c * 8:(c + 1) * 8],
                                          g1_sb[:, c:c + 1], None, ALU.mult), inc="vM")
            em.wait("vector", "vM", m)
            for c in grp:
                m = em.op("vector", lambda E, c=c:
                          E.scalar_tensor_tensor(lgB[:, (c % 4) * 8:(c % 4) * 8 + 8],
                                                 msk2_sb[:, c * 8:(c + 1) * 8],
                                                 g2_sb[:, c:c + 1],
                                                 lgB[:, (c % 4) * 8:(c % 4) * 8 + 8],
                                                 ALU.mult, ALU.add), inc="vM")
            em.wait("vector", "vM", m)
            for c in grp:
                mk[("comb", c)] = em.op("vector", lambda E, c=c:
                                        E.tensor_mul(comb_sb[:, c * 8:(c + 1) * 8],
                                                     lgB[:, (c % 4) * 8:(c % 4) * 8 + 8],
                                                     alpha_sb[:, :]), inc="vM")
            for c in grp:
                em.wait("tensor", "vM", mk[("comb", c)])
                em.guardw("tensor", ("bank", 6))
                tmk = em.op("tensor", lambda E, c=c:
                            E.transpose(ps[0:8, 6 * 512:6 * 512 + 128], comb_sb[:, c * 8:(c + 1) * 8],
                                        ident[:, :]), inc="tM")
                em.wait("vector", "tM", tmk)
                emk = em.op("vector", lambda E, c=c:
                            E.tensor_copy(combT_sb[:, c * 128:(c + 1) * 128],
                                          ps[0:8, 6 * 512:6 * 512 + 128]), inc="vM")
                em.consumed(("bank", 6), "vM", emk)

        M8 = RC[0:8, 0:2048]
        RNK = RC[0:8, 2048:4096]
        SCN8 = RC[0:8, 4096:6144]
        ONE8 = RC[0:8, 6144:8192]
        YTOK = RC[:, 8448:8576]
        MTOK = RC[:, 8576:8704]
        P1 = rt_sb[:, 0:32]
        P2 = rt_sb[:, 32:64]
        PPS = [RC[:, 8788 + i * 8: 8796 + i * 8] for i in range(2)]
        PPC = [RC[:, 8804 + i: 8805 + i] for i in range(3)]
        PP1F = RC[:, 8812:8828]
        PP2F = RC[:, 8828:8844]
        EIDB = RC[:, 8846:8870]
        CNT8 = RC[0:8, 8870:8871]
        N8 = RC[0:8, 8871:8872]
        TB8 = RC[0:8, 8872:8873]
        TN8 = RC[0:8, 8873:8874]
        TB256 = RC[0:8, 8874:8875]
        CM8x8 = RC[0:8, 8875:8883]
        CMPA = RC[0:8, 8907:8931]
        CMPB = RC[0:8, 8931:8955]
        EQM = RC[0:8, 8955:8979]
        PIO3 = RC[0:8, 8979:8982]
        R18A = RC[0:1, 8982:8990]
        R18B = RC[0:1, 8990:8998]
        R18C = RC[0:1, 8998:9006]
        TT3 = RC[0:24, 9006:9009]
        W24_4 = RC[0:1, 9009:9033]
        T3R0 = RC[0:1, 9033:9057]
        WSTG = RC[:, 9057:9081]
        ZT = RC[:, 9200:9298]
        TOKT = rt_sb[:, 512:528]
        PIO = rt_sb[0:128, 528:529]
        IOT24 = rt_sb[0:8, 529:553]
        PP1I = rt_sb[:, 432:448].bitcast(I32)
        PP2I = rt_sb[:, 448:464].bitcast(I32)
        WIDX = rt_sb[:, 464:488].bitcast(I32)
        BIDX = rt_sb[:, 488:512].bitcast(I32)

        def vf(f):
            m = em.op("vector", f, inc="vM")
            em.wait("vector", "vM", m)
            return m

        em.wait("vector", "tM", em.cnt["tM"])
        em.wait("vector", "sM", em.cnt["sM"])
        for ch in ("cSS0", "cSS1", "cX0", "cX1", "cSP0", "cSP1"):
            if em.cnt[ch]:
                em.wait("vector", ch, em.cnt[ch])
        vf(lambda E: E.memset(rt_sb[:, 0:512], 0.0))
        vf(lambda E: E.tensor_scalar(MTOK, comb_sb[:, :], 0.0, None, ALU.not_equal))
        vf(lambda E: E.tensor_scalar(M8, combT_sb[:, :], 0.0, None, ALU.not_equal))
        vf(lambda E: E.memset(ONE8, 1.0))
        vf(lambda E: E.tensor_tensor_scan(SCN8, ONE8, M8, 0.0, ALU.mult, ALU.add))
        vf(lambda E: E.tensor_tensor(RNK, SCN8, M8, ALU.subtract))
        vf(lambda E: E.reduce_sum(CNT8, M8, axis=AX.X))
        vf(lambda E: E.tensor_scalar(CM8x8, IOT24[:, 0:8], 256.0, None, ALU.mult))
        vf(lambda E: E.tensor_scalar(CM8x8, CM8x8, CNT8, None, ALU.is_lt))
        n8mk = vf(lambda E: E.reduce_sum(N8, CM8x8, axis=AX.X))
        em.guardw("tensor", ("bank", 0))
        em.wait("tensor", "vM", n8mk)
        tmk = em.op("tensor", lambda E: E.transpose(ps[0:1, 0:8], N8, ident[0:8, 0:8]), inc="tM")
        em.wait("vector", "tM", tmk)
        m_ = vf(lambda E: E.tensor_copy(R18A, ps[0:1, 0:8]))
        em.consumed(("bank", 0), "vM", m_)
        vf(lambda E: E.memset(R18B, 1.0))
        vf(lambda E: E.tensor_tensor_scan(R18C, R18B, R18A, 0.0, ALU.mult, ALU.add))
        rexmk = vf(lambda E: E.tensor_tensor(R18C, R18C, R18A, ALU.subtract))
        em.wait("tensor", "vM", rexmk)
        em.guardw("tensor", ("bank", 1))
        tmk = em.op("tensor", lambda E: E.transpose(ps[0:8, 512:513], R18C, ident[0:1, 0:1]), inc="tM")
        em.wait("vector", "tM", tmk)
        m_ = vf(lambda E: E.tensor_copy(TB8, ps[0:8, 512:513]))
        em.consumed(("bank", 1), "vM", m_)
        vf(lambda E: E.tensor_tensor(TN8, TB8, N8, ALU.add))
        vf(lambda E: E.tensor_scalar(TB256, TB8, 256.0, None, ALU.mult))
        vf(lambda E: E.tensor_scalar(CMPA, IOT24[:, 0:24], TB8, None, ALU.is_ge))
        vf(lambda E: E.tensor_scalar(CMPB, IOT24[:, 0:24], TN8, None, ALU.is_ge))
        vf(lambda E: E.tensor_tensor(EQM, CMPA, CMPB, ALU.subtract))
        vf(lambda E: E.tensor_copy(PIO3[:, 0:1], PIO[0:8, 0:1]))
        vf(lambda E: E.tensor_copy(PIO3[:, 1:2], TB8))
        eqmk = vf(lambda E: E.memset(PIO3[:, 2:3], 1.0))
        em.wait("tensor", "vM", eqmk)
        em.guardw("tensor", ("bank", 1))
        tmk = em.op("tensor", lambda E: E.matmul(ps[0:24, 512:515], EQM, PIO3,
                                                 start=True, stop=True), inc="tM")
        em.wait("vector", "tM", tmk)
        ttmk = vf(lambda E: E.tensor_copy(TT3, ps[0:24, 512:515]))
        em.consumed(("bank", 1), "vM", ttmk)
        em.wait("tensor", "vM", ttmk)
        em.guardw("tensor", ("bank", 0))
        tmk = em.op("tensor", lambda E: E.transpose(ps[0:1, 0:24], TT3[:, 0:1],
                                                    ident[0:24, 0:24]), inc="tM")
        em.wait("vector", "tM", tmk)
        m_ = vf(lambda E: E.tensor_copy(T3R0, ps[0:1, 0:24]))
        em.consumed(("bank", 0), "vM", m_)
        w4mk = vf(lambda E: E.tensor_scalar(W24_4, T3R0, 1.0, None, ALU.mult))
        em.guardw("tensor", ("bank", 2))
        em.wait("tensor", "vM", w4mk)
        tmk = em.op("tensor", lambda E: E.matmul(ps[:, 2 * 512: 2 * 512 + 24], ones128[0:1, 0:128],
                                                 W24_4, start=True, stop=True), inc="tM")
        em.wait("vector", "tM", tmk)
        m_ = vf(lambda E: E.tensor_copy(EIDB, ps[:, 2 * 512: 2 * 512 + 24]))
        em.consumed(("bank", 2), "vM", m_)
        vf(lambda E: E.tensor_scalar(WSTG, EIDB, 2048.0, PIO, ALU.mult, ALU.add))
        vf(lambda E: E.tensor_copy(WIDX, WSTG))
        vf(lambda E: E.tensor_scalar(WSTG, EIDB, 128.0, PIO, ALU.mult, ALU.add))
        vf(lambda E: E.tensor_copy(BIDX, WSTG))
        ypmk = vf(lambda E: E.tensor_scalar(SCN8, RNK, TB256, None, ALU.add))
        for c in range(NCH):
            em.wait("tensor", "vM", ypmk)
            em.guardw("tensor", ("bank", 2 + c % 2))
            t2 = em.op("tensor", lambda E, c=c: E.transpose(ps[0:128, (2 + c % 2) * 512:(2 + c % 2) * 512 + 8],
                                                            SCN8[:, c * 128:(c + 1) * 128],
                                                            ident[0:8, 0:8]), inc="tM")
            em.wait("vector", "tM", t2)
            e2 = em.op("vector", lambda E, c=c: E.tensor_copy(YTOK[:, c * 8:(c + 1) * 8],
                                                              ps[0:128, (2 + c % 2) * 512:(2 + c % 2) * 512 + 8]),
                       inc="vM")
            em.consumed(("bank", 2 + c % 2), "vM", e2)
        ztmk = vf(lambda E: E.memset(ZT, 0.0))
        em.wait("sync", "vM", ztmk)
        em.op("sync", lambda E: E.dma_start(
            gl2_dram[0:6144, :].rearrange("(p k) c -> p (k c)", p=128), ZT[:, 0:96]),
            inc="cGW", n=16)
        zdmk = em.op("sync", lambda E: E.dma_start(gl2_dram[6144:6145, :], ZT[0:1, 96:98]),
                     inc="cGW", n=16)
        CMB = [RC[:, 8844 + i: 8845 + i] for i in range(2)]
        for c in range(NCH):
            for which, mskp, ppf, pbuf in ((0, msk1_sb, PP1F, P1), (1, msk2_sb, PP2F, P2)):
                vf(lambda E, c=c, mskp=mskp: E.tensor_tensor(PPS[0], mskp[:, c * 8:(c + 1) * 8],
                                                             MTOK[:, c * 8:(c + 1) * 8], ALU.mult))
                vf(lambda E, c=c: E.tensor_tensor(PPS[1], PPS[0], YTOK[:, c * 8:(c + 1) * 8], ALU.mult))
                vf(lambda E: E.reduce_sum(PPC[0], PPS[1], axis=AX.X))
                vf(lambda E: E.reduce_sum(PPC[1], PPS[0], axis=AX.X))
                vf(lambda E: E.tensor_scalar(PPC[2], PPC[1], -6144.0, 6144.0, ALU.mult, ALU.add))
                vf(lambda E, c=c, ppf=ppf: E.tensor_tensor(ppf[:, c:c + 1], PPC[0], PPC[2], ALU.add))
                vf(lambda E, c=c, ppf=ppf: E.tensor_scalar(ppf[:, c:c + 1], ppf[:, c:c + 1],
                                                           6144.0, None, ALU.min))
                vf(lambda E, c=c, ppf=ppf: E.tensor_scalar(ppf[:, c:c + 1], ppf[:, c:c + 1],
                                                           0.0, None, ALU.max))
                vf(lambda E, c=c, mskp=mskp: E.tensor_tensor(PPS[1], PPS[0],
                                                             comb_sb[:, c * 8:(c + 1) * 8], ALU.mult))
                vf(lambda E, which=which: E.reduce_sum(CMB[which], PPS[1], axis=AX.X))
                vf(lambda E, c=c, pbuf=pbuf: E.tensor_scalar(pbuf[:, c * 2: c * 2 + 1],
                                                             TOKT[:, c:c + 1], 1.0, None, ALU.mult))
                vf(lambda E, c=c, which=which, pbuf=pbuf:
                   E.tensor_scalar(pbuf[:, c * 2 + 1: c * 2 + 2], CMB[which], 1.0, None, ALU.mult))
        vf(lambda E: E.tensor_copy(PP1I, PP1F))
        pimk = vf(lambda E: E.tensor_copy(PP2I, PP2F))
        em.wait("gpsimd", "cGW", zdmk)
        em.wait("gpsimd", "vM", pimk)
        for i in range(2 * NCH):
            c, which = i // 2, i % 2
            ch = "cGA%d" % (i % 2)
            em.guardw("gpsimd", ("sct", i % 2))
            smk = em.op("gpsimd", lambda E, c=c, which=which:
                        E.indirect_dma_start(
                            gl2_dram,
                            bass.IndirectOffsetOnAxis((PP1I if which == 0 else PP2I)[:, c:c + 1], 0),
                            (P1 if which == 0 else P2)[:, c * 2: c * 2 + 2], None),
                        inc=ch, n=16)
            em.consumed(("sct", i % 2), ch, smk)
        if debug_spills:
            em.wait("sync", "vM", em.cnt["vM"])
            em.op("sync", lambda E: E.dma_start(dbg_rt[:, 0:512], rt_sb[:, 0:512]), inc="cPRE", n=16)
        em.flush(nc, defer=("cGA0", "cGA1"))
        if bisect == "ph0":
            emit_dummy_out()
            return nc

        VXT = [rc(i * 1040, (i + 1) * 1040) for i in range(4)]
        F2T = [rc(4160 + i * 512, 4160 + (i + 1) * 512) for i in range(4)]
        LWIN = [rc(6208 + i * 1024, 6208 + (i + 1) * 1024) for i in range(4)]

        def laT(w, fc, col0, ncol):
            base = w * 4096 + fc * 512 + col0
            return RA[:, base: base + ncol]

        for kc in range(4):
            em.op("vector", lambda E, kc=kc: E.memset(VXT[kc][:, 64::65], 1.0))
        for w in range(NWIN):
            for kc in range(4):
                em.guardw("vector", ("vxt", kc))
            for kc in range(4):
                c = 4 * w + kc
                for h in range(NH):
                    lastc = (kc == 3 and h == NH - 1)
                    v = em.op("vector", lambda E, c=c, kc=kc, h=h:
                              E.tensor_copy(VXT[kc][:, h * 65: h * 65 + 64],
                                            RA[:, c * 1024 + h * 64: c * 1024 + (h + 1) * 64]),
                              inc="vM" if lastc else None)
            MKVXT = v
            for h in range(NH):
                pb = (h % 2) * 64
                hc = h // 2
                xTh = RB[pb:pb + 64, hc * 2048: (hc + 1) * 2048]
                for qc in range(4):
                    em.guardw("tensor", ("bank", qc))
                    smk = em.op("tensor", lambda E, w=w, qc=qc, xTh=xTh:
                                E.matmul(bank(qc, 512),
                                         xTh[:, w * 512 + qc * 128: w * 512 + (qc + 1) * 128],
                                         xTh[:, w * 512: (w + 1) * 512], start=True, stop=True),
                                inc="tM")
                    em.wait("scalar", "tM", smk)
                    em.guardw("scalar", ("f2", qc))
                    fmk = em.op("scalar", lambda E, qc=qc:
                                E.activation(F2T[qc], bank(qc, 512), AF.Exp, bias=c20n, scale=0.125),
                                inc="sM")
                    em.consumed(("bank", qc), "sM", fmk)
                mk[("F2", w, h)] = fmk
                em.wait("tensor", "vM", MKVXT)
                for qc in range(4):
                    em.wait("tensor", "sM", mk[("F2", w, h)])
                    em.guardw("tensor", ("bank", 4 + qc % 2))
                    for kc in range(4):
                        lastmm = kc == 3
                        v = em.op("tensor", lambda E, qc=qc, kc=kc, h=h:
                                  E.matmul(bank(4 + qc % 2, 65),
                                           F2T[kc][:, qc * 128:(qc + 1) * 128],
                                           VXT[kc][:, h * 65:(h + 1) * 65],
                                           start=(kc == 0), stop=(kc == 3)),
                                  inc="tM" if lastmm else None)
                    avmk = v
                    if qc == 3:
                        for kc in range(4):
                            em.consumed(("f2", kc), "tM", avmk)
                        if h == NH - 1:
                            for kc in range(4):
                                em.consumed(("vxt", kc), "tM", avmk)
                    em.wait("vector", "tM", avmk)
                    if h == 0:
                        em.guardw("vector", ("lwin", qc))
                    rlmk = em.op("vector", lambda E, qc=qc:
                                 E.reciprocal(sc(SC_RL), bank(4 + qc % 2, 1, off=64)), inc="vM")
                    em.wait("vector", "vM", rlmk)
                    emk = em.op("vector", lambda E, qc=qc, h=h:
                                E.tensor_scalar(LWIN[qc][:, h * 64:(h + 1) * 64], bank(4 + qc % 2, 64),
                                                sc(SC_RL), None, ALU.mult), inc="vM")
                    em.consumed(("bank", 4 + qc % 2), "vM", emk)
                    mk[("AVe", w, qc)] = emk
            for qc in range(4):
                for fc in range(FCN):
                    em.wait("tensor", "vM", mk[("AVe", w, qc)])
                    em.guardw("tensor", ("bank", 6 + fc % 2))
                    tmk = em.op("tensor", lambda E, qc=qc, fc=fc:
                                E.transpose(bank(6 + fc % 2, 128), LWIN[qc][:, fc * 128:(fc + 1) * 128],
                                            ident[:, :]), inc="tM")
                    em.wait("vector", "tM", tmk)
                    emk = em.op("vector", lambda E, w=w, qc=qc, fc=fc:
                                E.tensor_copy(laT(w, fc, qc * 128, 128), bank(6 + fc % 2, 128)),
                                inc="vM")
                    em.consumed(("bank", 6 + fc % 2), "vM", emk)
                em.consumed(("lwin", qc), "tM", tmk)
        if debug_spills:
            em.wait("sync", "vM", em.cnt["vM"])
            em.op("sync", lambda E: E.dma_start(dbg_local, RA[:, :]), inc="cPRE", n=16)
        em.flush(nc)

        F3T = [rc(i * 512, (i + 1) * 512) for i in range(2)]
        GWIN = [rc(1024 + i * 1024, 1024 + (i + 1) * 1024) for i in range(4)]

        for st in range(4):
            for h in range(NH):
                pb = (h % 2) * 64
                hc = h // 2
                em.guardw("tensor", ("bank", h % 2))
                s2mk = em.op("tensor", lambda E, st=st, pb=pb, hc=hc, h=h:
                             E.matmul(bank(h % 2, 512),
                                      gtT_sb[pb:pb + 64, hc * 128:(hc + 1) * 128],
                                      RB[pb:pb + 64, hc * 2048 + st * 512: hc * 2048 + (st + 1) * 512],
                                      start=True, stop=True), inc="tM")
                mk[("S2", st, h)] = s2mk
                em.wait("scalar", "tM", s2mk)
                em.guardw("scalar", ("f3", h % 2))
                fmk = em.op("scalar", lambda E, h=h:
                            E.activation(F3T[h % 2], bank(h % 2, 512), AF.Exp, bias=c20n, scale=0.125),
                            inc="sM")
                em.consumed(("bank", h % 2), "sM", fmk)
                em.wait("tensor", "sM", fmk)
                em.wait("tensor", "vM", MKGTX)
                for sco in range(4):
                    em.guardw("tensor", ("bank", 2 + sco % 2))
                    gvmk = em.op("tensor", lambda E, h=h, sco=sco:
                                 E.matmul(bank(2 + sco % 2, 65), F3T[h % 2][:, sco * 128:(sco + 1) * 128],
                                          gtx_sb[:, h * 65:(h + 1) * 65], start=True, stop=True),
                                 inc="tM")
                    if sco == 3:
                        em.consumed(("f3", h % 2), "tM", gvmk)
                    em.wait("vector", "tM", gvmk)
                    if h == 0:
                        em.guardw("vector", ("gwin", sco))
                    rlmk = em.op("vector", lambda E, sco=sco:
                                 E.reciprocal(sc(SC_RL), bank(2 + sco % 2, 1, off=64)), inc="vM")
                    em.wait("vector", "vM", rlmk)
                    emk = em.op("vector", lambda E, sco=sco, h=h:
                                E.tensor_scalar(GWIN[sco][:, h * 64:(h + 1) * 64],
                                                bank(2 + sco % 2, 64), sc(SC_RL), None, ALU.mult),
                                inc="vM")
                    em.consumed(("bank", 2 + sco % 2), "vM", emk)
                    mk[("GVe", st, sco)] = emk
            for sco in range(4):
                for fc in range(FCN):
                    em.wait("tensor", "vM", mk[("GVe", st, sco)])
                    em.guardw("tensor", ("bank", 4 + fc % 2))
                    tmk = em.op("tensor", lambda E, sco=sco, fc=fc:
                                E.transpose(bank(4 + fc % 2, 128), GWIN[sco][:, fc * 128:(fc + 1) * 128],
                                            ident[:, :]), inc="tM")
                    em.wait("vector", "tM", tmk)
                    emk = em.op("vector", lambda E, st=st, sco=sco, fc=fc:
                                E.tensor_copy(RB[:, fc * 2048 + st * 512 + sco * 128:
                                                 fc * 2048 + st * 512 + (sco + 1) * 128],
                                              bank(4 + fc % 2, 128)), inc="vM")
                    em.consumed(("bank", 4 + fc % 2), "vM", emk)
                em.consumed(("gwin", sco), "tM", tmk)
        if debug_spills:
            em.wait("sync", "vM", em.cnt["vM"])
            em.op("sync", lambda E: E.dma_start(dbg_glob, RB[:, :]), inc="cPRE", n=16)
        em.flush(nc)

        GWC = rc(0, 4096)
        MIXS = [rc(4096 + i * 2048, 4096 + (i + 1) * 2048) for i in range(2)]
        SSL = [rc(8192 + i * 512, 8192 + (i + 1) * 512) for i in range(2)]
        OWR = rc(0, 8192)

        def catT(i, tt):
            if i < 8:
                return laT(tt, i, 0, 512)
            return RB[:, (i - 8) * 2048 + tt * 512: (i - 8) * 2048 + (tt + 1) * 512]

        gidx = 0
        for jcp in range(4):
            em.guardw("sync", "rc_gw")
            src = gw_d.rearrange("(i p) j -> p i j", p=128)[:, :, jcp * 256:(jcp + 1) * 256]
            gwmk = em.op("sync", lambda E, s=src:
                         E.dma_start(GWC.rearrange("p (i c) -> p i c", c=256), s), inc="cGW", n=16)
            for jl in range(2):
                jc = jcp * 2 + jl
                for tt in range(4):
                    em.wait("tensor", "cGW", gwmk)
                    em.guardw("tensor", ("bank", gidx % 2))
                    for i in range(16):
                        lastmm = i == 15
                        v = em.op("tensor", lambda E, i=i, jl=jl, tt=tt, b=gidx % 2:
                                  E.matmul(bank(b, 512),
                                           GWC[:, i * 256 + jl * 128: i * 256 + (jl + 1) * 128],
                                           catT(i, tt), start=(i == 0), stop=(i == 15)),
                                  inc="tM" if lastmm else None)
                    gtmm = v
                    em.consumed("rc_gw", "tM", gtmm)
                    em.wait("scalar", "tM", gtmm)
                    if tt == 0:
                        em.guardw("scalar", ("mixs", jc % 2))
                    sgmk = em.op("scalar", lambda E, tt=tt, jc=jc, b=gidx % 2:
                                 E.activation(MIXS[jc % 2][:, tt * 512:(tt + 1) * 512], bank(b, 512),
                                              AF.Sigmoid, bias=gateb_sb[:, jc:jc + 1], scale=1.0),
                                 inc="sM")
                    em.consumed(("bank", gidx % 2), "sM", sgmk)
                    gidx += 1
                em.wait("gpsimd", "sM", sgmk)
                mspk = em.op("gpsimd", lambda E, jc=jc:
                             E.dma_start(mix_sp[:, jc * 2048:(jc + 1) * 2048], MIXS[jc % 2]),
                             inc="cMX%d" % (jc % 2), n=16)
                mk[("mxsp", jc)] = mspk
                em.consumed(("mixs", jc % 2), "cMX%d" % (jc % 2), mspk)

        for jc in range(FCN):
            em.guardw("gpsimd", ("mixs", jc % 2))
            em.wait("gpsimd", "cMX%d" % (jc % 2), mk[("mxsp", jc)])
            mrk = em.op("gpsimd", lambda E, jc=jc:
                        E.dma_start(MIXS[jc % 2], mix_sp[:, jc * 2048:(jc + 1) * 2048]),
                        inc="cGA%d" % (jc % 2), n=16)
            em.wait("vector", "cGA%d" % (jc % 2), mrk)
            if jc == 0:
                em.wait("vector", "tM", em.cnt["tM"])
            lT = RA.rearrange("p (w f t) -> p w f t", f=8, t=512)[:, :, jc, :]
            gT = RB[:, jc * 2048:(jc + 1) * 2048].rearrange("p (w t) -> p w t", t=512)
            mx = MIXS[jc % 2].rearrange("p (w t) -> p w t", t=512)
            em.op("vector", lambda E, lT=lT, gT=gT: E.tensor_tensor(lT, lT, gT, ALU.subtract))
            em.op("vector", lambda E, lT=lT, mx=mx: E.tensor_tensor(lT, mx, lT, ALU.mult))
            cmbmk = em.op("vector", lambda E, lT=lT, gT=gT:
                          E.tensor_tensor(lT, lT, gT, ALU.add), inc="vM")
            em.consumed(("mixs", jc % 2), "vM", cmbmk)
            em.consumed("rc_gw2", "vM", cmbmk)
            mk[("cmb", jc)] = cmbmk

        if debug_spills:
            em.wait("sync", "vM", mk[("cmb", 7)])
            em.op("sync", lambda E: E.dma_start(dbg_mixed, RA[:, :]), inc="cPRE", n=16)
        em.guardw("sync", "rc_gw")
        em.guardw("sync", "rc_gw2")
        owmk = em.op("sync", lambda E: E.dma_start(
            OWR.rearrange("p (k c) -> p k c", c=1024),
            ow_d.rearrange("(k p) c -> p k c", p=128)), inc="cGW", n=16)
        sidx = 0
        for tc in range(NCH):
            w = tc // 4
            for ot in range(2):
                em.wait("tensor", "cGW", owmk)
                em.wait("tensor", "vM", mk[("cmb", 7)])
                em.guardw("tensor", ("bank", 2 + sidx % 2))
                for k in range(8):
                    lastmm = k == 7
                    lhs = RA[:, w * 4096 + k * 512 + (tc % 4) * 128: w * 4096 + k * 512 + (tc % 4 + 1) * 128]
                    v = em.op("tensor", lambda E, k=k, ot=ot, lhs=lhs, b=2 + sidx % 2:
                              E.matmul(bank(b, 512), lhs,
                                       OWR[:, k * 1024 + ot * 512: k * 1024 + (ot + 1) * 512],
                                       start=(k == 0), stop=(k == 7)),
                              inc="tM" if lastmm else None)
                em.wait("vector", "tM", v)
                em.guardw("vector", ("ssl", sidx % 2))
                samk = em.op("vector", lambda E, ot=ot, b=2 + sidx % 2, sl=SSL[sidx % 2]:
                             E.tensor_add(sl, bank(b, 512), outb_sb[:, ot * 512:(ot + 1) * 512]),
                             inc="vM")
                em.consumed(("bank", 2 + sidx % 2), "vM", samk)
                em.wait("gpsimd", "vM", samk)
                spmk = em.op("gpsimd", lambda E, tc=tc, ot=ot, sl=SSL[sidx % 2]:
                             E.dma_start(s_sp[tc * 128:(tc + 1) * 128, ot * 512:(ot + 1) * 512], sl),
                             inc="cSS%d" % (sidx % 2), n=16)
                em.consumed(("ssl", sidx % 2), "cSS%d" % (sidx % 2), spmk)
                sidx += 1
        em.flush(nc)
        if bisect == "ph3":
            emit_dummy_out()
            return nc

        GELA = RB[:, 0:8192]
        W1R = [RB[:, 8192 + i * 2048: 8192 + (i + 1) * 2048] for i in range(2)]
        W2R = [RB[:, 12288 + i * 2048: 12288 + (i + 1) * 2048] for i in range(2)]
        GATH = [rc(i * 1024, (i + 1) * 1024) for i in range(2)]
        LBG = rc(2048, 4096)
        HDN = rc(4096, 6144)
        YSC = [rc(6144 + i * 2048, 6144 + (i + 1) * 2048) for i in range(2)]
        BT = [rc(10240 + i * 48, 10288 + i * 48) for i in range(2)]
        IDXT = [rc(10336 + i * 2, 10338 + i * 2).bitcast(I32) for i in range(2)]
        TSC = [rc(10344 + i * 4, 10348 + i * 4) for i in range(2)]

        def gel(fc, col0=0, n=256):
            return GELA[:, fc * 256 + col0: fc * 256 + col0 + n]

        NT = 23
        tile_js = list(range(NT))
        statics = ""
        if bisect is not None and ";static=" in bisect:
            bisect, statics = bisect.split(";static=")
        if bisect is not None and bisect.startswith("ph4t"):
            tile_js = list(range(int(bisect[4:])))
        elif bisect is not None and bisect.startswith("ph4j:"):
            tile_js = [int(x) for x in bisect[5:].split(",")]

        def emit_fc2s(j, fc):
            gp = j * 16 + fc // 2
            em.wait("tensor", "sM", mk[("gelu", j, fc)])
            em.wait("tensor", "cX%d" % ((fc // 2) % 2), mk[("w2", gp)])
            v2 = None
            for tcl in range(2):
                for hn in range(2):
                    b = 2 + 2 * tcl + hn
                    if fc == 0:
                        em.guardw("tensor", ("bank", b))
                    lastm = tcl == 1 and hn == 1
                    v2 = em.op("tensor", lambda E, fc=fc, tcl=tcl, hn=hn, b=b:
                               E.matmul(bank(b, 512), gel(fc, tcl * 128, 128),
                                        W2R[(fc // 2) % 2][:, (fc % 2) * 1024 + hn * 512:
                                                           (fc % 2) * 1024 + (hn + 1) * 512],
                                        start=(fc == 0), stop=(fc == NFC - 1)),
                               inc="tM" if lastm else None)
            mk[("fc2", j, fc)] = v2
            em.consumed(("gel", fc), "tM", v2)
            if fc % 2 == 1:
                em.consumed(("w2r_dma", (fc // 2) % 2), "tM", v2)
            return v2

        for j in tile_js:
            par = j % 2
            em.guardw("gpsimd", ("tsc", par))
            tscmk = em.op("gpsimd", lambda E, j=j, par=par:
                          E.dma_start(TSC[par].rearrange("p (k c) -> p k c", c=2),
                                      gl2_dram[j * 256:(j + 1) * 256, :]
                                      .rearrange("(k p) c -> p k c", p=128)),
                          inc="cGA%d" % par, n=16)
            mk[("tsc", j)] = tscmk
            em.guardw("gpsimd", ("bt", par))
            if "b" in statics:
                btmk = em.op("gpsimd", lambda E, par=par:
                             E.dma_start(BT[par], btab_d[0:128, :]), inc="cGA%d" % par, n=16)
            else:
                btmk = em.op("gpsimd", lambda E, j=j, par=par:
                             E.indirect_dma_start(BT[par], None, btab_d,
                                                  bass.IndirectOffsetOnAxis(BIDX[:, j:j + 1], 0)),
                             inc="cGA%d" % par, n=16)
            em.wait("vector", "cGA%d" % par, tscmk)
            em.guardw("vector", ("idxt", par))
            m2_ = em.op("vector", lambda E, par=par:
                        E.tensor_copy(IDXT[par], TSC[par][:, 0:4:2]), inc="vM")
            mk[("idxt", j)] = m2_
            em.wait("gpsimd", "vM", mk[("idxt", j)])
            for k in range(2):
                em.guardw("gpsimd", ("gath", par, k))
                if "l" in statics:
                    gamk = em.op("gpsimd", lambda E, k=k:
                                 E.dma_start(GATH[k], lb_tok_sp[0:128, :]),
                                 inc="cGA%d" % par, n=16)
                else:
                    gamk = em.op("gpsimd", lambda E, k=k, par=par:
                                 E.indirect_dma_start(GATH[k], None, lb_tok_sp,
                                                      bass.IndirectOffsetOnAxis(IDXT[par][:, k:k + 1], 0)),
                                 inc="cGA%d" % par, n=16)
                mk[("ga", j, k)] = gamk
            em.consumed(("idxt", par), "cGA%d" % par, gamk)
            for k in range(2):
                em.wait("tensor", "cGA%d" % par, mk[("ga", j, k)])
            em.guardw("tensor", "lbgbuf")
            for fc in range(FCN):
                for k in range(2):
                    em.guardw("tensor", ("bank", 6 + (fc * 2 + k) % 2))
                    t1 = em.op("tensor", lambda E, fc=fc, k=k, b=6 + (fc * 2 + k) % 2:
                               E.transpose(bank(b, 128), GATH[k][:, fc * 128:(fc + 1) * 128],
                                           ident[:, :]), inc="tM")
                    em.wait("vector", "tM", t1)
                    e1 = em.op("vector", lambda E, fc=fc, k=k, b=6 + (fc * 2 + k) % 2:
                               E.tensor_copy(LBG[:, fc * 256 + k * 128: fc * 256 + (k + 1) * 128],
                                             bank(b, 128)), inc="vM")
                    em.consumed(("bank", 6 + (fc * 2 + k) % 2), "vM", e1)
                    if fc == FCN - 1 and k == 1:
                        mk[("lbg", j)] = e1
            for k in range(2):
                em.consumed(("gath", par, k), "tM", t1)
            em.wait("vector", "cGA%d" % par, btmk)
            em.guardw("vector", "hdn")
            for fc in range(FCN):
                lastv = fc == FCN - 1
                hv = em.op("vector", lambda E, fc=fc, par=par:
                           E.tensor_scalar(HDN[:, fc * 256:(fc + 1) * 256],
                                           LBG[:, fc * 256:(fc + 1) * 256],
                                           BT[par][:, fc:fc + 1], BT[par][:, 8 + fc:9 + fc],
                                           ALU.mult, ALU.add), inc="vM" if lastv else None)
            MKHDN = hv
            em.consumed("lbgbuf", "vM", MKHDN)
            em.consumed(("bt", par), "vM", MKHDN)
            em.wait("tensor", "vM", MKHDN)
            if j >= 1:
                em.wait("tensor", "vM", mk.get(("ysc", j - 1)))
            for fc in range(NFC):
                if fc % 2 == 0:
                    pr = fc // 2
                    gp = j * 16 + pr
                    em.guardw("gpsimd", ("w1r", pr % 2))
                    if "w" in statics:
                        mk[("w1", gp)] = em.op("gpsimd", lambda E, pr=pr:
                                               E.dma_start(W1R[pr % 2],
                                                           w1l_d[pr * 128:(pr + 1) * 128, :]),
                                               inc="cW2%d" % (pr % 2), n=16)
                    else:
                        mk[("w1", gp)] = em.op("gpsimd", lambda E, j=j, pr=pr:
                                               E.indirect_dma_start(W1R[pr % 2], None, w1l_d,
                                                                    bass.IndirectOffsetOnAxis(WIDX[:, j:j + 1], 0),
                                                                    element_offset=pr * 262144),
                                               inc="cW2%d" % (pr % 2), n=16)
                    em.guardw("gpsimd", ("w2r_dma", pr % 2))
                    if "w" in statics:
                        mk[("w2", gp)] = em.op("gpsimd", lambda E, pr=pr:
                                               E.dma_start(W2R[pr % 2],
                                                           w2_d[pr * 128:(pr + 1) * 128, :]),
                                               inc="cX%d" % (pr % 2), n=16)
                    else:
                        mk[("w2", gp)] = em.op("gpsimd", lambda E, j=j, pr=pr:
                                               E.indirect_dma_start(W2R[pr % 2], None, w2_d,
                                                                    bass.IndirectOffsetOnAxis(WIDX[:, j:j + 1], 0),
                                                                    element_offset=pr * 262144),
                                               inc="cX%d" % (pr % 2), n=16)
                em.wait("tensor", "cW2%d" % ((fc // 2) % 2), mk[("w1", j * 16 + fc // 2)])
                em.guardw("tensor", ("bank", fc % 2))
                for jj in range(FCN):
                    lastm = jj == FCN - 1
                    fv = em.op("tensor", lambda E, fc=fc, jj=jj:
                               E.matmul(bank(fc % 2, 256),
                                        W1R[(fc // 2) % 2][:, (fc % 2) * 1024 + jj * 128:
                                                           (fc % 2) * 1024 + (jj + 1) * 128],
                                        HDN[:, jj * 256:(jj + 1) * 256],
                                        start=(jj == 0), stop=(jj == FCN - 1)),
                               inc="tM" if lastm else None)
                mk[("fc1", j, fc)] = fv
                if fc % 2 == 1:
                    em.consumed(("w1r", (fc // 2) % 2), "tM", fv)
                em.wait("scalar", "tM", mk[("fc1", j, fc)])
                em.guardw("scalar", ("gel", fc))
                gmk = em.op("scalar", lambda E, fc=fc, par=par:
                            E.activation(gel(fc), bank(fc % 2, 256), AF.Gelu_apprx_tanh,
                                         bias=BT[par][:, 16 + fc: 17 + fc], scale=1.0), inc="sM")
                mk[("gelu", j, fc)] = gmk
                em.consumed(("bank", fc % 2), "sM", gmk)
                if fc >= 1:
                    emit_fc2s(j, fc - 1)
            emit_fc2s(j, NFC - 1)
            em.consumed("hdn", "tM", mk[("fc1", j, NFC - 1)])
            em.consumed(("bt", par), "sM", mk[("gelu", j, NFC - 1)])
            em.wait("vector", "tM", mk[("fc2", j, NFC - 1)])
            em.guardw("vector", ("yscb", par))
            for tcl in range(2):
                for hn in range(2):
                    b = 2 + 2 * tcl + hn
                    yv = em.op("vector", lambda E, tcl=tcl, hn=hn, b=b, par=par:
                               E.tensor_scalar(YSC[par][:, tcl * 1024 + hn * 512:
                                                        tcl * 1024 + (hn + 1) * 512],
                                               bank(b, 512),
                                               TSC[par][:, 1 + 2 * tcl: 2 + 2 * tcl], None,
                                               ALU.mult), inc="vM")
                    em.consumed(("bank", b), "vM", yv)
            mk[("ysc", j)] = yv
            em.consumed(("tsc", par), "vM", yv)
            em.wait("sync", "vM", mk[("ysc", j)])
            for k in range(2):
                ysmk = em.op("sync", lambda E, j=j, k=k, par=par:
                             E.dma_start(y_dram[j * 256 + k * 128: j * 256 + (k + 1) * 128, :],
                                         YSC[par][:, k * 1024:(k + 1) * 1024]),
                             inc="cSP%d" % par, n=16)
            em.consumed(("yscb", par), "cSP%d" % par, ysmk)
        zmk = em.op("vector", lambda E: E.memset(rc(0, 2048), 0.0), inc="vM")
        em.wait("sync", "vM", zmk)
        em.op("sync", lambda E: E.dma_start(
            y_dram[NT * 256:6144, :].rearrange("(p k) c -> p (k c)", p=128), rc(0, 2048)),
            inc="cPRE", n=16)
        em.op("sync", lambda E: E.dma_start(y_dram[6144:6145, :], rc(0, 1024)[0:1, :]),
              inc="cPRE", n=16)
        em.flush(nc)
        if bisect is not None and bisect.startswith("ph4"):
            emit_dummy_out()
            return nc

        G1 = [RB[:, i * 1024: (i + 1) * 1024] for i in range(4)]
        G2 = [RB[:, 4096 + i * 1024: 4096 + (i + 1) * 1024] for i in range(4)]
        XF = [RB[:, 8192 + i * 1024: 8192 + (i + 1) * 1024] for i in range(4)]
        SF = [RB[:, 12288 + i * 1024: 12288 + (i + 1) * 1024] for i in range(4)]
        PD = 3

        def ph5_issue(c):
            q = c % 4
            em.guardw("gpsimd", ("pf", q))
            mk[("xl", c)] = em.op("gpsimd", lambda E, c=c, q=q:
                                  E.dma_start(XF[q], x_d[c * 128:(c + 1) * 128, :]),
                                  inc="cX%d" % (c % 2), n=16)
            mk[("sl", c)] = em.op("gpsimd", lambda E, c=c, q=q:
                                  E.dma_start(SF[q], s_sp[c * 128:(c + 1) * 128, :]),
                                  inc="cSS%d" % (c % 2), n=16)
            em.guardw("gpsimd", ("gb", q))
            em.op("gpsimd", lambda E, c=c, q=q:
                  E.indirect_dma_start(G1[q], None, y_dram,
                                       bass.IndirectOffsetOnAxis(PP1I[:, c:c + 1], 0)),
                  inc="cGA%d" % (c % 2), n=16)
            mk[("g2", c)] = em.op("gpsimd", lambda E, c=c, q=q:
                                  E.indirect_dma_start(G2[q], None, y_dram,
                                                       bass.IndirectOffsetOnAxis(PP2I[:, c:c + 1], 0)),
                                  inc="cGA%d" % (c % 2), n=16)
            for hn in range(2):
                b = (c % 4) * 2 + hn
                em.guardw("tensor", ("bank", b))
                mk[("bc", c, hn)] = em.op("tensor", lambda E, c=c, hn=hn, b=b:
                                          E.matmul(bank(b, 512), combT_sb[:, c * 128:(c + 1) * 128],
                                                   f2b_sb[:, hn * 512:(hn + 1) * 512],
                                                   start=True, stop=True), inc="tM")

        def ph5_compute(c):
            q = c % 4
            em.wait("vector", "cGA%d" % (c % 2), mk[("g2", c)])
            em.op("vector", lambda E, q=q: E.tensor_tensor(G1[q], G1[q], G2[q], ALU.add))
            for hn in range(2):
                b = (c % 4) * 2 + hn
                em.wait("vector", "tM", mk[("bc", c, hn)])
                av = em.op("vector", lambda E, q=q, hn=hn, b=b:
                           E.tensor_tensor(G1[q][:, hn * 512:(hn + 1) * 512],
                                           G1[q][:, hn * 512:(hn + 1) * 512],
                                           bank(b, 512), ALU.add), inc="vM")
                em.consumed(("bank", b), "vM", av)
            em.wait("scalar", "vM", av)
            simk = em.op("scalar", lambda E, q=q:
                         E.activation(G1[q], G1[q], AF.Silu), inc="sM")
            em.wait("vector", "sM", simk)
            em.wait("vector", "cSS%d" % (c % 2), mk[("sl", c)])
            em.wait("vector", "cX%d" % (c % 2), mk[("xl", c)])
            em.op("vector", lambda E, q=q: E.tensor_tensor(SF[q], SF[q], G1[q], ALU.mult))
            fmk = em.op("vector", lambda E, q=q:
                        E.tensor_tensor(SF[q], SF[q], XF[q], ALU.add), inc="vM")
            em.consumed(("gb", q), "vM", fmk)
            em.wait("sync", "vM", fmk)
            omk = em.op("sync", lambda E, c=c, q=q:
                        E.dma_start(out_d[c * 128:(c + 1) * 128, :], SF[q]),
                        inc="cSP%d" % (c % 2), n=16)
            em.consumed(("pf", q), "cSP%d" % (c % 2), omk)

        for c in range(NCH + PD):
            if c >= PD:
                ph5_compute(c - PD)
            if c < NCH:
                ph5_issue(c)
        em.flush(nc)

    return nc


def kernel(**inputs):
    X = np.ascontiguousarray(np.asarray(inputs["L0_attn_x"], dtype=np.float32))
    P = np.ascontiguousarray(np.asarray(inputs["L0_attn_pos_emb"], dtype=np.float32))
    GM = np.asarray(inputs["L0_attn_global_memory"], dtype=np.float32)
    CW = np.asarray(inputs["L0_attn_conv_w"], dtype=np.float32)
    CB = np.asarray(inputs["L0_attn_conv_b"], dtype=np.float32)
    GWm = np.ascontiguousarray(np.asarray(inputs["L0_attn_gate_w"], dtype=np.float32))
    GB = np.asarray(inputs["L0_attn_gate_b"], dtype=np.float32)
    OW = np.ascontiguousarray(np.asarray(inputs["L0_attn_out_w"], dtype=np.float32))
    OB = np.asarray(inputs["L0_attn_out_b"], dtype=np.float32)
    WG = np.ascontiguousarray(np.asarray(inputs["L0_ffn_Wg"], dtype=np.float32))
    AL = np.asarray(inputs["L0_ffn_alpha"], dtype=np.float32)
    LNW = np.asarray(inputs["L0_ffn_ln_w"], dtype=np.float32)
    LNB = np.asarray(inputs["L0_ffn_ln_b"], dtype=np.float32)
    W1 = np.asarray(inputs["L0_ffn_fc1_w"], dtype=np.float32)
    B1 = np.asarray(inputs["L0_ffn_fc1_b"], dtype=np.float32)
    W2 = np.ascontiguousarray(np.asarray(inputs["L0_ffn_fc2_w"], dtype=np.float32))
    B2 = np.ascontiguousarray(np.asarray(inputs["L0_ffn_fc2_b"], dtype=np.float32))

    gmT = np.ascontiguousarray(GM[0].T)
    cwl = np.ascontiguousarray(CW.transpose(1, 2, 0))
    gateb_t = np.ascontiguousarray(GB.reshape(8, 128).T)
    outb_t = np.ascontiguousarray(np.broadcast_to(OB, (128, D)))
    wgv = np.ascontiguousarray(WG.reshape(8, 128, 8).transpose(1, 0, 2).reshape(128, 64))
    alpha_t = np.ascontiguousarray(np.broadcast_to(AL, (128, 8)))
    lnw_t = np.ascontiguousarray(LNW.reshape(8, 8, 128).transpose(2, 0, 1).reshape(128, 64))
    lnb_t = np.ascontiguousarray(LNB.reshape(8, 8, 128).transpose(2, 0, 1).reshape(128, 64))
    w1l = np.ascontiguousarray(
        W1.reshape(8, 8, 128, 32, 128).transpose(0, 3, 2, 1, 4).reshape(8, 32, 128, 1024)
        .reshape(8, 16, 2, 128, 1024).transpose(0, 1, 3, 2, 4).reshape(8 * 16 * 128, 2048))
    w2l = np.ascontiguousarray(
        W2.reshape(8, 16, 2, 128, 1024).transpose(0, 1, 3, 2, 4).reshape(8 * 16 * 128, 2048))
    f1b_t = np.ascontiguousarray(B1.reshape(8, 32, 128).transpose(2, 0, 1).reshape(128, 256))
    convb_t = np.ascontiguousarray(CB.reshape(8, 128).T)
    identm = np.eye(128, dtype=np.float32)
    iot = np.ascontiguousarray(np.broadcast_to(np.arange(2048, dtype=np.float32), (128, 2048)))
    tokt = (np.arange(16, dtype=np.float32)[None, :] * 128
            + np.arange(128, dtype=np.float32)[:, None]).astype(np.float32)
    pio = np.arange(128, dtype=np.float32).reshape(128, 1)
    btab = np.zeros((8, 128, 48), dtype=np.float32)
    lnw_r = LNW.reshape(8, 8, 128)
    lnb_r = LNB.reshape(8, 8, 128)
    f1b_r = B1.reshape(8, 32, 128)
    for e in range(8):
        btab[e, :, 0:8] = lnw_r[e].T
        btab[e, :, 8:16] = lnb_r[e].T
        btab[e, :, 16:48] = f1b_r[e].T
    btab = np.ascontiguousarray(btab.reshape(1024, 48))

    nc = build()
    in_maps = []
    for core in range(8):
        b, j = core // 4, core % 4
        in_maps.append({
            "x": np.ascontiguousarray(X[b, j * TOK:(j + 1) * TOK]),
            "x0": np.ascontiguousarray(X[b, 0:256]),
            "pos": np.ascontiguousarray(P[j * TOK:(j + 1) * TOK]),
            "pos0": np.ascontiguousarray(P[0:256]),
            "gmT": gmT, "cwl": cwl, "gw": GWm, "gateb": gateb_t,
            "ow": OW, "outb": outb_t, "wgv": wgv, "alpha": alpha_t,
            "lnw": lnw_t, "lnb": lnb_t, "w1l": w1l, "w2": w2l,
            "f1b": f1b_t, "f2b": B2, "convb": convb_t, "ident": identm,
            "iot": iot, "tokt": tokt, "pio": pio, "btab": btab,
        })
    res = run_bass_kernel_spmd(nc, in_maps, core_ids=list(range(8)))
    out = np.empty((2, 8192, D), dtype=np.float32)
    for core in range(8):
        b, j = core // 4, core % 4
        out[b, j * TOK:(j + 1) * TOK] = res.results[core]["out"]
    return out


# version 15
# speedup vs baseline: 1.1339x; 1.1339x over previous
import numpy as np
from contextlib import ExitStack

import concourse.bass as bass
import concourse.mybir as mybir
from concourse.bass_utils import run_bass_kernel_spmd

F32 = mybir.dt.float32
I32 = mybir.dt.int32
AF = mybir.ActivationFunctionType
ALU = mybir.AluOpType
AX = mybir.AxisListType

TOK = 2048
NCH = 16
D = 1024
FCN = 8
NH = 16
NWIN = 4
NEXP = 8
FF = 4096
NFC = 32
TTK = 256
NTT = TOK // TTK
C0 = 20.0
EPS = 1e-5

ENGS = ("sync", "gpsimd", "vector", "scalar", "tensor")


class Em:

    def __init__(self, sems, dma_ch):
        self.sems = sems
        self.dma_ch = dma_ch
        self.cnt = {k: 0 for k in sems}
        self.q = {e: [] for e in ENGS}
        self.last = {}

    def op(self, eng, f, inc=None, n=1):
        if inc is not None:
            self.cnt[inc] += n
            v = self.cnt[inc]
            h = self.sems[inc]

            def g(E, f=f, h=h, n=n):
                f(E).then_inc(h, n)

            self.q[eng].append(g)
            return v
        self.q[eng].append(lambda E, f=f: f(E))
        return None

    def wait(self, eng, sem, val):
        if val is None or val <= 0:
            return
        h = self.sems[sem]
        self.q[eng].append(lambda E, h=h, v=int(val): E.wait_ge(h, v))

    def consumed(self, key, sem, val):
        d = self.last.setdefault(key, {})
        d[sem] = max(d.get(sem, 0), val)

    def guardw(self, eng, key):
        d = self.last.pop(key, None)
        if d:
            for s, v in d.items():
                self.wait(eng, s, v)

    def flush(self, nc, defer=()):
        for ch, eng in self.dma_ch.items():
            if self.cnt[ch] and ch not in defer:
                self.wait(eng, ch, self.cnt[ch])
        with nc.Block() as blk:
            for e in ENGS:
                if self.q[e]:
                    getattr(blk, e)(lambda E, lst=tuple(self.q[e]): [g(E) for g in lst])
        self.q = {e: [] for e in ENGS}
        self.last = {}


def build(debug_spills=False, bisect=None):
    nc = bass.Bass("TRN2", target_bir_lowering=False, detect_race_conditions=False)

    x_d = nc.dram_tensor("x", [TOK, D], F32, kind="ExternalInput").ap()
    x0_d = nc.dram_tensor("x0", [256, D], F32, kind="ExternalInput").ap()
    pos_d = nc.dram_tensor("pos", [TOK, D], F32, kind="ExternalInput").ap()
    pos0_d = nc.dram_tensor("pos0", [256, D], F32, kind="ExternalInput").ap()
    gmT_d = nc.dram_tensor("gmT", [D, 64], F32, kind="ExternalInput").ap()
    cwl_d = nc.dram_tensor("cwl", [D, 4, D], F32, kind="ExternalInput").ap()
    gw_d = nc.dram_tensor("gw", [2 * D, D], F32, kind="ExternalInput").ap()
    gateb_d = nc.dram_tensor("gateb", [128, 8], F32, kind="ExternalInput").ap()
    ow_d = nc.dram_tensor("ow", [D, D], F32, kind="ExternalInput").ap()
    outb_d = nc.dram_tensor("outb", [128, D], F32, kind="ExternalInput").ap()
    wgv_d = nc.dram_tensor("wgv", [128, 64], F32, kind="ExternalInput").ap()
    alpha_d = nc.dram_tensor("alpha", [128, 8], F32, kind="ExternalInput").ap()
    lnw_d = nc.dram_tensor("lnw", [128, 64], F32, kind="ExternalInput").ap()
    lnb_d = nc.dram_tensor("lnb", [128, 64], F32, kind="ExternalInput").ap()
    w1l_d = nc.dram_tensor("w1l", [NEXP * 16 * 128, 2048], F32, kind="ExternalInput").ap()
    w2_d = nc.dram_tensor("w2", [NEXP * 16 * 128, 2048], F32, kind="ExternalInput").ap()
    f1b_d = nc.dram_tensor("f1b", [128, NEXP * 32], F32, kind="ExternalInput").ap()
    f2b_d = nc.dram_tensor("f2b", [NEXP, D], F32, kind="ExternalInput").ap()
    convb_d = nc.dram_tensor("convb", [128, 8], F32, kind="ExternalInput").ap()
    ident_d = nc.dram_tensor("ident", [128, 128], F32, kind="ExternalInput").ap()
    out_d = nc.dram_tensor("out", [TOK, D], F32, kind="ExternalOutput").ap()

    spill_kind = "ExternalOutput" if debug_spills else "Internal"
    lb_tok_sp = nc.dram_tensor("lb_tok_sp", [TOK, D], F32, kind=spill_kind).ap()
    s_sp = nc.dram_tensor("s_sp", [TOK, D], F32, kind=spill_kind).ap()
    mix_sp = nc.dram_tensor("mix_sp", [128, NCH * 1024], F32, kind="Internal").ap()
    y_dram = nc.dram_tensor("y_dram", [24 * 256 + 1, D], F32,
                            kind="ExternalOutput" if debug_spills else "Internal").ap()
    gl2_dram = nc.dram_tensor("gl2_dram", [24 * 256 + 1, 2], F32, kind="Internal").ap()
    iot_d = nc.dram_tensor("iot", [128, 2048], F32, kind="ExternalInput").ap()
    tok_d = nc.dram_tensor("tokt", [128, 16], F32, kind="ExternalInput").ap()
    pio_d = nc.dram_tensor("pio", [128, 1], F32, kind="ExternalInput").ap()
    btab_d = nc.dram_tensor("btab", [NEXP * 128, 48], F32, kind="ExternalInput").ap()
    if debug_spills:
        dbg_local = nc.dram_tensor("dbg_local", [128, 16384], F32, kind="ExternalOutput").ap()
        dbg_glob = nc.dram_tensor("dbg_glob", [128, 16384], F32, kind="ExternalOutput").ap()
        dbg_mixed = nc.dram_tensor("dbg_mixed", [128, 16384], F32, kind="ExternalOutput").ap()
        dbg_rt = nc.dram_tensor("dbg_rt", [128, 560], F32, kind="ExternalOutput").ap()

    with ExitStack() as ctx:
        def sbuf(name, rows, cols):
            return nc.alloc_sbuf_tensor(name, [rows, cols], F32).ap()

        RA = sbuf("RA", 128, 16384)
        RB = sbuf("RB", 128, 16384)
        RC = sbuf("RC", 128, 11264)
        ident = sbuf("ident_s", 128, 128)
        ones128 = sbuf("ones128", 128, 128)
        outb_sb = sbuf("outb_sb", 128, D)
        alpha_sb = sbuf("alpha_sb", 128, 8)
        wgsum_sb = sbuf("wgsum_sb", 128, 8)
        convb_sb = sbuf("convb_sb", 128, 8)
        gateb_sb = sbuf("gateb_sb", 128, 8)
        lnw_sb = sbuf("lnw_sb", 128, 64)
        lnb_sb = sbuf("lnb_sb", 128, 64)
        f1b_sb = sbuf("f1b_sb", 128, NEXP * 32)
        f2b_sb = sbuf("f2b_sb", 8, D)
        wg_sb = sbuf("wg_sb", 128, 64)
        combT_sb = sbuf("combT_sb", 8, TOK)
        comb_sb = sbuf("comb_sb", 128, NCH * 8)
        gtT_sb = sbuf("gtT_sb", 128, D)
        gtx_sb = sbuf("gtx_sb", 128, NH * 65)
        mu2_sb = sbuf("mu2_sb", 128, NCH)
        std2_sb = sbuf("std2_sb", 128, NCH)
        d12_sb = sbuf("d12_sb", 128, NCH)
        g1_sb = sbuf("g1_sb", 128, NCH)
        g2_sb = sbuf("g2_sb", 128, NCH)
        msk1_sb = sbuf("msk1_sb", 128, NCH * 8)
        msk2_sb = sbuf("msk2_sb", 128, NCH * 8)
        eps_t = sbuf("eps_t", 128, 1)
        c20n = sbuf("c20n", 128, 1)
        scr = sbuf("scr", 128, 96)
        lgA = sbuf("lgA", 128, 64)
        lgB = sbuf("lgB", 128, 32)
        rt_sb = sbuf("rt_sb", 128, 560)

        ps = nc.alloc_psum_tensor("ps", [128, 4096], F32).ap()

        def bank(b, n=512, off=0):
            return ps[:, b * 512 + off: b * 512 + off + n]

        dma_ch = {"cPRE": "sync", "cSP0": "sync", "cSP1": "sync", "cGW": "sync",
                  "cW10": "sync", "cW11": "sync", "cBT0": "sync", "cBT1": "sync",
                  "cX0": "gpsimd", "cX1": "gpsimd", "cSS0": "gpsimd", "cSS1": "gpsimd",
                  "cMX0": "gpsimd", "cMX1": "gpsimd", "cW20": "gpsimd", "cW21": "gpsimd",
                  "cGA0": "gpsimd", "cGA1": "gpsimd"}
        sems = {}
        for name in list(dma_ch) + ["tM", "sM", "vM"]:
            sems[name] = ctx.enter_context(nc.semaphore(name))
        em = Em(sems, dma_ch)
        mk = {}

        SC_SUM, SC_MNEG, SC_SQS, SC_V, SC_STD0, SC_R0, SC_Q, SC_S1, SC_STD1, SC_RSTD = range(10)
        SC_M1, SC_M2, SC_RL, SC_T1 = 10, 11, 12, 13

        def sc(i, g=0):
            j = g * 10 + i
            return scr[:, j:j + 1]

        def rc(a, b):
            return RC[:, a:b]

        def emit_dummy_out():
            DOF = [rc(i * 1024, (i + 1) * 1024) for i in range(2)]
            for c in range(NCH):
                par = c % 2
                em.guardw("gpsimd", ("dxf", par))
                xl = em.op("gpsimd", lambda E, c=c, par=par:
                           E.dma_start(DOF[par], x_d[c * 128:(c + 1) * 128, :]),
                           inc="cX%d" % par, n=16)
                em.wait("sync", "cX%d" % par, xl)
                om = em.op("sync", lambda E, c=c, par=par:
                           E.dma_start(out_d[c * 128:(c + 1) * 128, :], DOF[par]),
                           inc="cSP%d" % par, n=16)
                em.consumed(("dxf", par), "cSP%d" % par, om)
            em.flush(nc)

        XSL = [rc(i * 1024, (i + 1) * 1024) for i in range(4)]
        LBS = [rc(4096 + i * 1024, 4096 + (i + 1) * 1024) for i in range(4)]
        SQS = rc(8192, 9216)
        LBT = [rc(9216 + i * 1024, 9216 + (i + 1) * 1024) for i in range(2)]
        XA0 = [LBS[0], LBS[1]]

        em.op("vector", lambda E: E.memset(eps_t, EPS))
        em.op("vector", lambda E: E.memset(c20n, -C0))
        mk["consts"] = em.op("vector", lambda E: E.memset(ones128[:, :], 1.0), inc="vM")

        for nm, dst, src in [
            ("ident", ident[:, :], ident_d),
            ("wg", wg_sb[:, :], wgv_d),
            ("alpha", alpha_sb[:, :], alpha_d),
            ("gateb", gateb_sb[:, :], gateb_d),
            ("convb", convb_sb[:, :], convb_d),
            ("lnw", lnw_sb[:, :], lnw_d),
            ("lnb", lnb_sb[:, :], lnb_d),
            ("f1b", f1b_sb[:, :], f1b_d),
            ("f2b", f2b_sb[:, :], f2b_d),
            ("outb", outb_sb[:, :], outb_d),
            ("gm", gtT_sb.rearrange("p (f g) -> p f g", g=128)[:, :, 64:128],
             gmT_d.rearrange("(f p) g -> p f g", p=128)),
            ("iot", rt_sb[0:8, 529:553], iot_d[0:8, 0:24]),
            ("tokt", rt_sb[:, 512:528], tok_d),
            ("pio", rt_sb[0:128, 528:529], pio_d),
        ]:
            mk[("pre", nm)] = em.op("sync", lambda E, d=dst, s=src: E.dma_start(d, s), inc="cPRE", n=16)
        PRE_ALL = mk[("pre", "gm")]

        em.wait("tensor", "cPRE", PRE_ALL)
        em.wait("tensor", "vM", mk["consts"])
        for kc in range(8):
            last = kc == 7
            v = em.op("tensor", lambda E, kc=kc:
                      E.matmul(bank(5, 8), ones128[:, :], wg_sb[:, kc * 8:(kc + 1) * 8],
                               start=(kc == 0), stop=(kc == 7)), inc="tM" if last else None)
        mk["wgsmm"] = v
        em.wait("vector", "tM", mk["wgsmm"])
        mk["wgs"] = em.op("vector", lambda E: E.tensor_copy(wgsum_sb[:, :], bank(5, 8)), inc="vM")
        em.consumed(("bank", 5), "vM", mk["wgs"])

        def ln_batch(jobs):
            def step(eng, fns, inc):
                m = None
                for f in fns:
                    m = em.op(eng, f, inc=inc)
                return m
            m = step("vector", [lambda E, s=s, g=g: E.reduce_sum(sc(SC_SUM, g), s, axis=AX.X)
                                for s, _, g, _, _ in jobs], "vM")
            em.wait("scalar", "vM", m)
            fns = []
            for s, _, g, _, mu in jobs:
                fns.append(lambda E, g=g: E.mul(sc(SC_MNEG, g), sc(SC_SUM, g), -1.0 / D))
                if mu is not None:
                    fns.append(lambda E, g=g, mu=mu: E.mul(mu, sc(SC_SUM, g), 1.0 / D))
            m = step("scalar", fns, "sM")
            em.wait("scalar", "sM", m)
            m = step("scalar", [lambda E, s=s, g=g:
                                E.activation(SQS, s, AF.Square, bias=sc(SC_MNEG, g), scale=1.0,
                                             accum_out=sc(SC_SQS, g)) for s, _, g, _, _ in jobs], "sM")
            sq_m = m
            em.wait("vector", "sM", m)
            m = step("vector", [lambda E, g=g: E.tensor_scalar(sc(SC_V, g), sc(SC_SQS, g), 1.0 / D,
                                                               EPS, ALU.mult, ALU.add)
                                for _, _, g, _, _ in jobs], "vM")
            em.wait("scalar", "vM", m)
            m = step("scalar", [lambda E, g=g: E.activation(sc(SC_STD0, g), sc(SC_V, g), AF.Sqrt)
                                for _, _, g, _, _ in jobs], "sM")
            em.wait("vector", "sM", m)
            m = step("vector", [lambda E, g=g: E.reciprocal(sc(SC_R0, g), sc(SC_STD0, g))
                                for _, _, g, _, _ in jobs], "vM")
            em.wait("vector", "vM", m)
            m = step("vector", [lambda E, g=g: E.tensor_mul(sc(SC_Q, g), sc(SC_V, g), sc(SC_R0, g))
                                for _, _, g, _, _ in jobs], "vM")
            em.wait("vector", "vM", m)
            m = step("vector", [lambda E, g=g: E.tensor_add(sc(SC_S1, g), sc(SC_STD0, g), sc(SC_Q, g))
                                for _, _, g, _, _ in jobs], "vM")
            em.wait("vector", "vM", m)
            sds = [sd if sd is not None else sc(SC_STD1, g) for _, _, g, sd, _ in jobs]
            m = step("vector", [lambda E, g=g, d=d: E.tensor_scalar_mul(d, sc(SC_S1, g), 0.5)
                                for (_, _, g, _, _), d in zip(jobs, sds)], "vM")
            em.wait("vector", "vM", m)
            m = step("vector", [lambda E, g=g, s=d: E.reciprocal(sc(SC_RSTD, g), s)
                                for (_, _, g, _, _), d in zip(jobs, sds)], "vM")
            em.wait("vector", "vM", m)
            return [em.op("vector", lambda E, s=s, d=d, g=g:
                          E.tensor_scalar(d, s, sc(SC_MNEG, g), sc(SC_RSTD, g), ALU.add, ALU.mult),
                          inc="vM") for s, d, g, _, _ in jobs], sq_m

        def scM(k):
            return scr[:, 80 + k: 81 + k]

        def emit_top2(batch):
            cs = [c for c in batch]

            def la(c):
                return lgA[:, ((c // 4) % 2) * 32 + (c % 4) * 8: ((c // 4) % 2) * 32 + (c % 4) * 8 + 8]

            def lb_(c):
                return lgB[:, (c % 4) * 8: (c % 4) * 8 + 8]
            m = None
            for c in cs:
                m = em.op("vector", lambda E, c=c: E.tensor_scalar(lb_(c), wgsum_sb[:, :],
                                                                   mu2_sb[:, c:c + 1], None,
                                                                   ALU.mult), inc="vM")
            em.wait("vector", "vM", m)
            for c in cs:
                m = em.op("vector", lambda E, c=c: E.tensor_add(la(c), la(c), lb_(c)), inc="vM")
            em.wait("vector", "vM", m)
            for c in cs:
                m = em.op("vector", lambda E, c=c: E.reduce_max(scM(c % 4), la(c), axis=AX.X), inc="vM")
            em.wait("vector", "vM", m)
            for c in cs:
                m = em.op("vector", lambda E, c=c: E.tensor_scalar(msk1_sb[:, c * 8:(c + 1) * 8],
                                                                   la(c), scM(c % 4), None,
                                                                   ALU.is_equal), inc="vM")
            em.wait("vector", "vM", m)
            for c in cs:
                m = em.op("vector", lambda E, c=c:
                          E.scalar_tensor_tensor(lb_(c), msk1_sb[:, c * 8:(c + 1) * 8], -1e30,
                                                 la(c), ALU.mult, ALU.add), inc="vM")
            em.wait("vector", "vM", m)
            for c in cs:
                m = em.op("vector", lambda E, c=c: E.reduce_max(scM(4 + c % 4), lb_(c), axis=AX.X),
                          inc="vM")
            em.wait("vector", "vM", m)
            for c in cs:
                m = em.op("vector", lambda E, c=c: E.tensor_scalar(msk2_sb[:, c * 8:(c + 1) * 8],
                                                                   lb_(c), scM(4 + c % 4), None,
                                                                   ALU.is_equal), inc="vM")
            em.wait("vector", "vM", m)
            for c in cs:
                m = em.op("vector", lambda E, c=c: E.tensor_tensor(d12_sb[:, c:c + 1], scM(c % 4),
                                                                   scM(4 + c % 4), ALU.subtract),
                          inc="vM")
                mk[("d12", c)] = m
            em.wait("vector", "vM", m)

        batches = [list(range(b * 4, min(b * 4 + 4, 18))) for b in range(5)]
        ti = 0
        for c in batches[0]:
            em.guardw("gpsimd", ("xsl", c % 4))
            mk[("x", c)] = em.op("gpsimd", lambda E, d=XSL[c % 4], s=x_d[c * 128:(c + 1) * 128, :]:
                                 E.dma_start(d, s), inc="cX%d" % (c % 2), n=16)
        for b, batch in enumerate(batches):
            em.wait("vector", "cX%d" % (batch[-1] % 2), mk[("x", batch[-1])])
            if len(batch) > 1:
                em.wait("vector", "cX%d" % (batch[-2] % 2), mk[("x", batch[-2])])
            if b == 4:
                for q in (0, 1):
                    em.guardw("vector", ("lbs", q))
            jobs = [(XSL[c % 4],
                     RA[:, c * 1024:(c + 1) * 1024] if c < 16 else XA0[c - 16],
                     c % 4, None, None) for c in batch]
            vms, sq1 = ln_batch(jobs)
            for c, v in zip(batch, vms):
                em.consumed(("xsl", c % 4), "vM", v)
                mk[("xn", c)] = v
            for c in batch:
                em.guardw("gpsimd", ("xsl", c % 4))
                psrc = pos_d[c * 128:(c + 1) * 128, :] if c < 16 else \
                    pos0_d[(c - 16) * 128:(c - 15) * 128, :]
                mk[("pl", c)] = em.op("gpsimd", lambda E, d=XSL[c % 4], s=psrc: E.dma_start(d, s),
                                      inc="cSS%d" % (c % 2), n=16)
            if b < 4:
                for c in batch:
                    em.guardw("vector", ("lbs", c % 4))
                jobs = [(RA[:, c * 1024:(c + 1) * 1024], LBS[c % 4], 4 + c % 4,
                         std2_sb[:, c:c + 1], mu2_sb[:, c:c + 1]) for c in batch]
                lbvms, sq2 = ln_batch(jobs)
                for c, v in zip(batch, lbvms):
                    mk[("lb", c)] = v
                em.wait("vector", "sM", sq2)
            else:
                em.wait("vector", "sM", sq1)
            for c in batch:
                em.wait("vector", "cSS%d" % (c % 2), mk[("pl", c)])
                dst = RA[:, c * 1024:(c + 1) * 1024] if c < 16 else XA0[c - 16]
                mk[("padd", c)] = em.op("vector", lambda E, d=dst, p=XSL[c % 4]:
                                        E.tensor_add(d, d, p), inc="vM")
                em.consumed(("xsl", c % 4), "vM", mk[("padd", c)])
            if b + 1 < len(batches):
                for c in batches[b + 1]:
                    em.guardw("gpsimd", ("xsl", c % 4))
                    src = x_d[c * 128:(c + 1) * 128, :] if c < 16 else \
                        x0_d[(c - 16) * 128:(c - 15) * 128, :]
                    mk[("x", c)] = em.op("gpsimd", lambda E, d=XSL[c % 4], s=src: E.dma_start(d, s),
                                         inc="cX%d" % (c % 2), n=16)
            if b == 4:
                break

            for c in batch:
                em.wait("sync", "vM", mk[("lb", c)])
                spmk = em.op("sync", lambda E, c=c: E.dma_start(lb_tok_sp[c * 128:(c + 1) * 128, :],
                                                                LBS[c % 4]),
                             inc="cSP%d" % (c % 2), n=16)
                em.consumed(("lbs", c % 4), "cSP%d" % (c % 2), spmk)

            for c in batch:
                em.wait("tensor", "vM", mk[("lb", c)])
                for fc in range(FCN):
                    em.guardw("tensor", ("bank", fc % 2))
                    if fc == 0:
                        em.guardw("tensor", ("lbt", c % 2))
                    tmk = em.op("tensor", lambda E, c=c, fc=fc:
                                E.transpose(bank(fc % 2, 128), LBS[c % 4][:, fc * 128:(fc + 1) * 128],
                                            ident[:, :]), inc="tM")
                    em.wait("vector", "tM", tmk)
                    emk = em.op("vector", lambda E, c=c, fc=fc:
                                E.tensor_copy(LBT[c % 2][:, fc * 128:(fc + 1) * 128], bank(fc % 2, 128)),
                                inc="vM")
                    em.consumed(("bank", fc % 2), "vM", emk)
                em.consumed(("lbs", c % 4), "tM", tmk)
                em.wait("tensor", "vM", emk)
                em.guardw("tensor", ("bank", 4 + c % 4))
                for fc in range(FCN):
                    last = fc == 7
                    v = em.op("tensor", lambda E, c=c, fc=fc:
                              E.matmul(bank(4 + c % 4, 8), LBT[c % 2][:, fc * 128:(fc + 1) * 128],
                                       wg_sb[:, fc * 8:(fc + 1) * 8], start=(fc == 0), stop=(fc == 7)),
                              inc="tM" if last else None)
                mk[("lgt", c)] = v
                em.consumed(("lbt", c % 2), "tM", v)
            em.wait("vector", "vM", mk["wgs"])
            m = None
            for c in batch:
                em.wait("vector", "tM", mk[("lgt", c)])
                m = em.op("vector", lambda E, c=c:
                          E.tensor_scalar(lgA[:, ((c // 4) % 2) * 32 + (c % 4) * 8:
                                              ((c // 4) % 2) * 32 + (c % 4) * 8 + 8],
                                          bank(4 + c % 4, 8), std2_sb[:, c:c + 1], None,
                                          ALU.mult), inc="vM")
                em.consumed(("bank", 4 + c % 4), "vM", m)
            em.wait("vector", "vM", m)
            for c in batch:
                for fc in range(FCN):
                    em.wait("tensor", "vM", mk[("padd", c)])
                    em.guardw("tensor", ("bank", 2 + ti % 2))
                    tmk = em.op("tensor", lambda E, c=c, fc=fc, b_=2 + ti % 2:
                                E.transpose(bank(b_, 128),
                                            RA[:, c * 1024 + fc * 128: c * 1024 + (fc + 1) * 128],
                                            ident[:, :]), inc="tM")
                    em.wait("vector", "tM", tmk)
                    emk = em.op("vector", lambda E, c=c, fc=fc, b_=2 + ti % 2:
                                E.tensor_copy(RB[:, fc * 2048 + c * 128: fc * 2048 + (c + 1) * 128],
                                              bank(b_, 128)), inc="vM")
                    em.consumed(("bank", 2 + ti % 2), "vM", emk)
                    ti += 1
            if b >= 1:
                emit_top2(batches[b - 1])
        emit_top2(batches[3])

        XA0T = rc(0, 2048)
        GTOK = rc(2048, 3072)
        CWR = [LBS[2], LBS[3], SQS, XSL[3]]
        em.guardw("tensor", ("xsl", 0))
        em.guardw("tensor", ("xsl", 1))
        for k in range(2):
            for fc in range(FCN):
                em.wait("tensor", "vM", mk[("padd", 16 + k)])
                em.guardw("tensor", ("bank", 2 + ti % 2))
                tmk = em.op("tensor", lambda E, k=k, fc=fc, b_=2 + ti % 2:
                            E.transpose(bank(b_, 128), XA0[k][:, fc * 128:(fc + 1) * 128], ident[:, :]),
                            inc="tM")
                em.wait("vector", "tM", tmk)
                emk = em.op("vector", lambda E, k=k, fc=fc, b_=2 + ti % 2:
                            E.tensor_copy(XA0T[:, fc * 256 + k * 128: fc * 256 + (k + 1) * 128],
                                          bank(b_, 128)), inc="vM")
                em.consumed(("bank", 2 + ti % 2), "vM", emk)
                em.consumed(("xsl", 0), "vM", emk)
                em.consumed(("xsl", 1), "vM", emk)
                ti += 1
        MKXA0T = emk

        em.wait("tensor", "vM", MKXA0T)
        for q in (2, 3):
            em.guardw("sync", ("lbs", q))
        em.guardw("sync", ("xsl", 3))
        em.wait("sync", "sM", em.cnt["sM"])
        flat = 0
        for p_ in range(2):
            for tidx, (k, ic) in enumerate([(k, ic) for k in range(4) for ic in range(8)]):
                em.guardw("sync", ("cwr", flat % 4))
                src = cwl_d[ic * 128:(ic + 1) * 128, k, :]
                cmk = em.op("sync", lambda E, d=CWR[flat % 4], s=src: E.dma_start(d, s),
                            inc=["cW10", "cW11", "cBT0", "cBT1"][flat % 4], n=16)
                em.wait("tensor", ["cW10", "cW11", "cBT0", "cBT1"][flat % 4], cmk)
                if tidx == 0:
                    for osl in range(4):
                        em.guardw("tensor", ("bank", 4 + osl))
                rhs = XA0T.rearrange("p (f t) -> p f t", t=256)[:, ic, :][:, k:256:4]
                for osl in range(4):
                    og = p_ * 4 + osl
                    lastmm = osl == 3
                    v = em.op("tensor", lambda E, flat=flat, osl=osl, og=og, tidx=tidx, rhs=rhs:
                              E.matmul(bank(4 + osl, 64), CWR[flat % 4][:, og * 128:(og + 1) * 128],
                                       rhs, start=(tidx == 0), stop=(tidx == 31)),
                              inc="tM" if lastmm else None)
                em.consumed(("cwr", flat % 4), "tM", v)
                flat += 1
            for osl in range(4):
                og = p_ * 4 + osl
                em.wait("vector", "tM", v)
                em.wait("vector", "cPRE", PRE_ALL)
                gmk = em.op("vector", lambda E, osl=osl, og=og:
                            E.tensor_scalar(gtT_sb[:, og * 128: og * 128 + 64], bank(4 + osl, 64),
                                            convb_sb[:, og:og + 1], None, ALU.add), inc="vM")
                em.consumed(("bank", 4 + osl), "vM", gmk)
        MKCONV = gmk

        for fc in range(FCN):
            em.wait("tensor", "vM", MKCONV)
            em.guardw("tensor", ("bank", 4 + fc % 2))
            tmk = em.op("tensor", lambda E, fc=fc:
                        E.transpose(bank(4 + fc % 2, 128), gtT_sb[:, fc * 128:(fc + 1) * 128],
                                    ident[:, :]), inc="tM")
            em.wait("vector", "tM", tmk)
            emk = em.op("vector", lambda E, fc=fc:
                        E.tensor_copy(GTOK[:, fc * 128:(fc + 1) * 128], bank(4 + fc % 2, 128)),
                        inc="vM")
            em.consumed(("bank", 4 + fc % 2), "vM", emk)
        em.op("vector", lambda E: E.memset(gtx_sb[:, 64::65], 1.0))
        for h in range(NH):
            gxmk = em.op("vector", lambda E, h=h:
                         E.tensor_copy(gtx_sb[:, h * 65: h * 65 + 64], GTOK[:, h * 64:(h + 1) * 64]),
                         inc="vM" if h == NH - 1 else None)
        MKGTX = gxmk

        em.wait("scalar", "vM", mk[("d12", 15)])
        em.op("scalar", lambda E: E.activation(g1_sb[:, :], d12_sb[:, :], AF.Sigmoid))
        mk["sig2"] = em.op("scalar", lambda E: E.activation(g2_sb[:, :], d12_sb[:, :], AF.Sigmoid,
                                                            scale=-1.0), inc="sM")
        em.wait("vector", "sM", mk["sig2"])
        for c4 in range(0, NCH, 4):
            grp = range(c4, c4 + 4)
            m = None
            for c in grp:
                m = em.op("vector", lambda E, c=c:
                          E.tensor_scalar(lgB[:, (c % 4) * 8:(c % 4) * 8 + 8],
                                          msk1_sb[:, c * 8:(c + 1) * 8],
                                          g1_sb[:, c:c + 1], None, ALU.mult), inc="vM")
            em.wait("vector", "vM", m)
            for c in grp:
                m = em.op("vector", lambda E, c=c:
                          E.scalar_tensor_tensor(lgB[:, (c % 4) * 8:(c % 4) * 8 + 8],
                                                 msk2_sb[:, c * 8:(c + 1) * 8],
                                                 g2_sb[:, c:c + 1],
                                                 lgB[:, (c % 4) * 8:(c % 4) * 8 + 8],
                                                 ALU.mult, ALU.add), inc="vM")
            em.wait("vector", "vM", m)
            for c in grp:
                mk[("comb", c)] = em.op("vector", lambda E, c=c:
                                        E.tensor_mul(comb_sb[:, c * 8:(c + 1) * 8],
                                                     lgB[:, (c % 4) * 8:(c % 4) * 8 + 8],
                                                     alpha_sb[:, :]), inc="vM")
            for c in grp:
                em.wait("tensor", "vM", mk[("comb", c)])
                em.guardw("tensor", ("bank", 6))
                tmk = em.op("tensor", lambda E, c=c:
                            E.transpose(ps[0:8, 6 * 512:6 * 512 + 128], comb_sb[:, c * 8:(c + 1) * 8],
                                        ident[:, :]), inc="tM")
                em.wait("vector", "tM", tmk)
                emk = em.op("vector", lambda E, c=c:
                            E.tensor_copy(combT_sb[:, c * 128:(c + 1) * 128],
                                          ps[0:8, 6 * 512:6 * 512 + 128]), inc="vM")
                em.consumed(("bank", 6), "vM", emk)

        M8 = RC[0:8, 0:2048]
        RNK = RC[0:8, 2048:4096]
        SCN8 = RC[0:8, 4096:6144]
        ONE8 = RC[0:8, 6144:8192]
        YTOK = RC[:, 8448:8576]
        MTOK = RC[:, 8576:8704]
        P1 = rt_sb[:, 0:32]
        P2 = rt_sb[:, 32:64]
        PPS = [RC[:, 8788 + i * 8: 8796 + i * 8] for i in range(2)]
        PPC = [RC[:, 8804 + i: 8805 + i] for i in range(3)]
        PP1F = RC[:, 8812:8828]
        PP2F = RC[:, 8828:8844]
        EIDB = RC[:, 8846:8870]
        CNT8 = RC[0:8, 8870:8871]
        N8 = RC[0:8, 8871:8872]
        TB8 = RC[0:8, 8872:8873]
        TN8 = RC[0:8, 8873:8874]
        TB256 = RC[0:8, 8874:8875]
        CM8x8 = RC[0:8, 8875:8883]
        CMPA = RC[0:8, 8907:8931]
        CMPB = RC[0:8, 8931:8955]
        EQM = RC[0:8, 8955:8979]
        PIO3 = RC[0:8, 8979:8982]
        R18A = RC[0:1, 8982:8990]
        R18B = RC[0:1, 8990:8998]
        R18C = RC[0:1, 8998:9006]
        TT3 = RC[0:24, 9006:9009]
        W24_4 = RC[0:1, 9009:9033]
        T3R0 = RC[0:1, 9033:9057]
        WSTG = RC[:, 9057:9081]
        ZT = RC[:, 9200:9298]
        TOKT = rt_sb[:, 512:528]
        PIO = rt_sb[0:128, 528:529]
        IOT24 = rt_sb[0:8, 529:553]
        PP1I = rt_sb[:, 432:448].bitcast(I32)
        PP2I = rt_sb[:, 448:464].bitcast(I32)
        WIDX = rt_sb[:, 464:488].bitcast(I32)
        BIDX = rt_sb[:, 488:512].bitcast(I32)

        def vf(f):
            m = em.op("vector", f, inc="vM")
            em.wait("vector", "vM", m)
            return m

        em.wait("vector", "tM", em.cnt["tM"])
        em.wait("vector", "sM", em.cnt["sM"])
        for ch in ("cSS0", "cSS1", "cX0", "cX1", "cSP0", "cSP1"):
            if em.cnt[ch]:
                em.wait("vector", ch, em.cnt[ch])
        vf(lambda E: E.memset(rt_sb[:, 0:512], 0.0))
        vf(lambda E: E.tensor_scalar(MTOK, comb_sb[:, :], 0.0, None, ALU.not_equal))
        vf(lambda E: E.tensor_scalar(M8, combT_sb[:, :], 0.0, None, ALU.not_equal))
        vf(lambda E: E.memset(ONE8, 1.0))
        vf(lambda E: E.tensor_tensor_scan(SCN8, ONE8, M8, 0.0, ALU.mult, ALU.add))
        vf(lambda E: E.tensor_tensor(RNK, SCN8, M8, ALU.subtract))
        vf(lambda E: E.reduce_sum(CNT8, M8, axis=AX.X))
        vf(lambda E: E.tensor_scalar(CM8x8, IOT24[:, 0:8], 256.0, None, ALU.mult))
        vf(lambda E: E.tensor_scalar(CM8x8, CM8x8, CNT8, None, ALU.is_lt))
        n8mk = vf(lambda E: E.reduce_sum(N8, CM8x8, axis=AX.X))
        em.guardw("tensor", ("bank", 0))
        em.wait("tensor", "vM", n8mk)
        tmk = em.op("tensor", lambda E: E.transpose(ps[0:1, 0:8], N8, ident[0:8, 0:8]), inc="tM")
        em.wait("vector", "tM", tmk)
        m_ = vf(lambda E: E.tensor_copy(R18A, ps[0:1, 0:8]))
        em.consumed(("bank", 0), "vM", m_)
        vf(lambda E: E.memset(R18B, 1.0))
        vf(lambda E: E.tensor_tensor_scan(R18C, R18B, R18A, 0.0, ALU.mult, ALU.add))
        rexmk = vf(lambda E: E.tensor_tensor(R18C, R18C, R18A, ALU.subtract))
        em.wait("tensor", "vM", rexmk)
        em.guardw("tensor", ("bank", 1))
        tmk = em.op("tensor", lambda E: E.transpose(ps[0:8, 512:513], R18C, ident[0:1, 0:1]), inc="tM")
        em.wait("vector", "tM", tmk)
        m_ = vf(lambda E: E.tensor_copy(TB8, ps[0:8, 512:513]))
        em.consumed(("bank", 1), "vM", m_)
        vf(lambda E: E.tensor_tensor(TN8, TB8, N8, ALU.add))
        vf(lambda E: E.tensor_scalar(TB256, TB8, 256.0, None, ALU.mult))
        vf(lambda E: E.tensor_scalar(CMPA, IOT24[:, 0:24], TB8, None, ALU.is_ge))
        vf(lambda E: E.tensor_scalar(CMPB, IOT24[:, 0:24], TN8, None, ALU.is_ge))
        vf(lambda E: E.tensor_tensor(EQM, CMPA, CMPB, ALU.subtract))
        vf(lambda E: E.tensor_copy(PIO3[:, 0:1], PIO[0:8, 0:1]))
        vf(lambda E: E.tensor_copy(PIO3[:, 1:2], TB8))
        eqmk = vf(lambda E: E.memset(PIO3[:, 2:3], 1.0))
        em.wait("tensor", "vM", eqmk)
        em.guardw("tensor", ("bank", 1))
        tmk = em.op("tensor", lambda E: E.matmul(ps[0:24, 512:515], EQM, PIO3,
                                                 start=True, stop=True), inc="tM")
        em.wait("vector", "tM", tmk)
        ttmk = vf(lambda E: E.tensor_copy(TT3, ps[0:24, 512:515]))
        em.consumed(("bank", 1), "vM", ttmk)
        em.wait("tensor", "vM", ttmk)
        em.guardw("tensor", ("bank", 0))
        tmk = em.op("tensor", lambda E: E.transpose(ps[0:1, 0:24], TT3[:, 0:1],
                                                    ident[0:24, 0:24]), inc="tM")
        em.wait("vector", "tM", tmk)
        m_ = vf(lambda E: E.tensor_copy(T3R0, ps[0:1, 0:24]))
        em.consumed(("bank", 0), "vM", m_)
        w4mk = vf(lambda E: E.tensor_scalar(W24_4, T3R0, 1.0, None, ALU.mult))
        em.guardw("tensor", ("bank", 2))
        em.wait("tensor", "vM", w4mk)
        tmk = em.op("tensor", lambda E: E.matmul(ps[:, 2 * 512: 2 * 512 + 24], ones128[0:1, 0:128],
                                                 W24_4, start=True, stop=True), inc="tM")
        em.wait("vector", "tM", tmk)
        m_ = vf(lambda E: E.tensor_copy(EIDB, ps[:, 2 * 512: 2 * 512 + 24]))
        em.consumed(("bank", 2), "vM", m_)
        vf(lambda E: E.tensor_scalar(WSTG, EIDB, 2048.0, PIO, ALU.mult, ALU.add))
        vf(lambda E: E.tensor_copy(WIDX, WSTG))
        vf(lambda E: E.tensor_scalar(WSTG, EIDB, 128.0, PIO, ALU.mult, ALU.add))
        vf(lambda E: E.tensor_copy(BIDX, WSTG))
        ypmk = vf(lambda E: E.tensor_scalar(SCN8, RNK, TB256, None, ALU.add))
        for c in range(NCH):
            em.wait("tensor", "vM", ypmk)
            em.guardw("tensor", ("bank", 2 + c % 2))
            t2 = em.op("tensor", lambda E, c=c: E.transpose(ps[0:128, (2 + c % 2) * 512:(2 + c % 2) * 512 + 8],
                                                            SCN8[:, c * 128:(c + 1) * 128],
                                                            ident[0:8, 0:8]), inc="tM")
            em.wait("vector", "tM", t2)
            e2 = em.op("vector", lambda E, c=c: E.tensor_copy(YTOK[:, c * 8:(c + 1) * 8],
                                                              ps[0:128, (2 + c % 2) * 512:(2 + c % 2) * 512 + 8]),
                       inc="vM")
            em.consumed(("bank", 2 + c % 2), "vM", e2)
        ztmk = vf(lambda E: E.memset(ZT, 0.0))
        em.wait("sync", "vM", ztmk)
        em.op("sync", lambda E: E.dma_start(
            gl2_dram[0:6144, :].rearrange("(p k) c -> p (k c)", p=128), ZT[:, 0:96]),
            inc="cGW", n=16)
        zdmk = em.op("sync", lambda E: E.dma_start(gl2_dram[6144:6145, :], ZT[0:1, 96:98]),
                     inc="cGW", n=16)
        CMB = [RC[:, 8844 + i: 8845 + i] for i in range(2)]
        for c in range(NCH):
            for which, mskp, ppf, pbuf in ((0, msk1_sb, PP1F, P1), (1, msk2_sb, PP2F, P2)):
                vf(lambda E, c=c, mskp=mskp: E.tensor_tensor(PPS[0], mskp[:, c * 8:(c + 1) * 8],
                                                             MTOK[:, c * 8:(c + 1) * 8], ALU.mult))
                vf(lambda E, c=c: E.tensor_tensor(PPS[1], PPS[0], YTOK[:, c * 8:(c + 1) * 8], ALU.mult))
                vf(lambda E: E.reduce_sum(PPC[0], PPS[1], axis=AX.X))
                vf(lambda E: E.reduce_sum(PPC[1], PPS[0], axis=AX.X))
                vf(lambda E: E.tensor_scalar(PPC[2], PPC[1], -6144.0, 6144.0, ALU.mult, ALU.add))
                vf(lambda E, c=c, ppf=ppf: E.tensor_tensor(ppf[:, c:c + 1], PPC[0], PPC[2], ALU.add))
                vf(lambda E, c=c, ppf=ppf: E.tensor_scalar(ppf[:, c:c + 1], ppf[:, c:c + 1],
                                                           6144.0, None, ALU.min))
                vf(lambda E, c=c, ppf=ppf: E.tensor_scalar(ppf[:, c:c + 1], ppf[:, c:c + 1],
                                                           0.0, None, ALU.max))
                vf(lambda E, c=c, mskp=mskp: E.tensor_tensor(PPS[1], PPS[0],
                                                             comb_sb[:, c * 8:(c + 1) * 8], ALU.mult))
                vf(lambda E, which=which: E.reduce_sum(CMB[which], PPS[1], axis=AX.X))
                vf(lambda E, c=c, pbuf=pbuf: E.tensor_scalar(pbuf[:, c * 2: c * 2 + 1],
                                                             TOKT[:, c:c + 1], 1.0, None, ALU.mult))
                vf(lambda E, c=c, which=which, pbuf=pbuf:
                   E.tensor_scalar(pbuf[:, c * 2 + 1: c * 2 + 2], CMB[which], 1.0, None, ALU.mult))
        vf(lambda E: E.tensor_copy(PP1I, PP1F))
        pimk = vf(lambda E: E.tensor_copy(PP2I, PP2F))
        em.wait("gpsimd", "cGW", zdmk)
        em.wait("gpsimd", "vM", pimk)
        for i in range(2 * NCH):
            c, which = i // 2, i % 2
            ch = "cGA%d" % (i % 2)
            em.guardw("gpsimd", ("sct", i % 2))
            smk = em.op("gpsimd", lambda E, c=c, which=which:
                        E.indirect_dma_start(
                            gl2_dram,
                            bass.IndirectOffsetOnAxis((PP1I if which == 0 else PP2I)[:, c:c + 1], 0),
                            (P1 if which == 0 else P2)[:, c * 2: c * 2 + 2], None),
                        inc=ch, n=16)
            em.consumed(("sct", i % 2), ch, smk)
        if debug_spills:
            em.wait("sync", "vM", em.cnt["vM"])
            em.op("sync", lambda E: E.dma_start(dbg_rt[:, 0:512], rt_sb[:, 0:512]), inc="cPRE", n=16)
        em.flush(nc, defer=("cGA0", "cGA1"))
        if bisect == "ph0":
            emit_dummy_out()
            return nc

        VXT = [rc(i * 1040, (i + 1) * 1040) for i in range(4)]
        F2T = [rc(4160 + i * 512, 4160 + (i + 1) * 512) for i in range(4)]
        LWIN = [rc(6208 + i * 1024, 6208 + (i + 1) * 1024) for i in range(4)]

        def laT(w, fc, col0, ncol):
            base = w * 4096 + fc * 512 + col0
            return RA[:, base: base + ncol]

        for kc in range(4):
            em.op("vector", lambda E, kc=kc: E.memset(VXT[kc][:, 64::65], 1.0))
        for w in range(NWIN):
            for kc in range(4):
                em.guardw("vector", ("vxt", kc))
            for kc in range(4):
                c = 4 * w + kc
                for h in range(NH):
                    lastc = (kc == 3 and h == NH - 1)
                    v = em.op("vector", lambda E, c=c, kc=kc, h=h:
                              E.tensor_copy(VXT[kc][:, h * 65: h * 65 + 64],
                                            RA[:, c * 1024 + h * 64: c * 1024 + (h + 1) * 64]),
                              inc="vM" if lastc else None)
            MKVXT = v
            for h in range(NH):
                pb = (h % 2) * 64
                hc = h // 2
                xTh = RB[pb:pb + 64, hc * 2048: (hc + 1) * 2048]
                for qc in range(4):
                    em.guardw("tensor", ("bank", qc))
                    smk = em.op("tensor", lambda E, w=w, qc=qc, xTh=xTh:
                                E.matmul(bank(qc, 512),
                                         xTh[:, w * 512 + qc * 128: w * 512 + (qc + 1) * 128],
                                         xTh[:, w * 512: (w + 1) * 512], start=True, stop=True),
                                inc="tM")
                    em.wait("scalar", "tM", smk)
                    em.guardw("scalar", ("f2", qc))
                    fmk = em.op("scalar", lambda E, qc=qc:
                                E.activation(F2T[qc], bank(qc, 512), AF.Exp, bias=c20n, scale=0.125),
                                inc="sM")
                    em.consumed(("bank", qc), "sM", fmk)
                mk[("F2", w, h)] = fmk
                em.wait("tensor", "vM", MKVXT)
                for qc in range(4):
                    em.wait("tensor", "sM", mk[("F2", w, h)])
                    em.guardw("tensor", ("bank", 4 + qc % 2))
                    for kc in range(4):
                        lastmm = kc == 3
                        v = em.op("tensor", lambda E, qc=qc, kc=kc, h=h:
                                  E.matmul(bank(4 + qc % 2, 65),
                                           F2T[kc][:, qc * 128:(qc + 1) * 128],
                                           VXT[kc][:, h * 65:(h + 1) * 65],
                                           start=(kc == 0), stop=(kc == 3)),
                                  inc="tM" if lastmm else None)
                    avmk = v
                    if qc == 3:
                        for kc in range(4):
                            em.consumed(("f2", kc), "tM", avmk)
                        if h == NH - 1:
                            for kc in range(4):
                                em.consumed(("vxt", kc), "tM", avmk)
                    em.wait("vector", "tM", avmk)
                    if h == 0:
                        em.guardw("vector", ("lwin", qc))
                    rlmk = em.op("vector", lambda E, qc=qc:
                                 E.reciprocal(sc(SC_RL), bank(4 + qc % 2, 1, off=64)), inc="vM")
                    em.wait("vector", "vM", rlmk)
                    emk = em.op("vector", lambda E, qc=qc, h=h:
                                E.tensor_scalar(LWIN[qc][:, h * 64:(h + 1) * 64], bank(4 + qc % 2, 64),
                                                sc(SC_RL), None, ALU.mult), inc="vM")
                    em.consumed(("bank", 4 + qc % 2), "vM", emk)
                    mk[("AVe", w, qc)] = emk
            for qc in range(4):
                for fc in range(FCN):
                    em.wait("tensor", "vM", mk[("AVe", w, qc)])
                    em.guardw("tensor", ("bank", 6 + fc % 2))
                    tmk = em.op("tensor", lambda E, qc=qc, fc=fc:
                                E.transpose(bank(6 + fc % 2, 128), LWIN[qc][:, fc * 128:(fc + 1) * 128],
                                            ident[:, :]), inc="tM")
                    em.wait("vector", "tM", tmk)
                    emk = em.op("vector", lambda E, w=w, qc=qc, fc=fc:
                                E.tensor_copy(laT(w, fc, qc * 128, 128), bank(6 + fc % 2, 128)),
                                inc="vM")
                    em.consumed(("bank", 6 + fc % 2), "vM", emk)
                em.consumed(("lwin", qc), "tM", tmk)
        if debug_spills:
            em.wait("sync", "vM", em.cnt["vM"])
            em.op("sync", lambda E: E.dma_start(dbg_local, RA[:, :]), inc="cPRE", n=16)
        em.flush(nc)

        F3T = [rc(i * 512, (i + 1) * 512) for i in range(2)]
        GWIN = [rc(1024 + i * 1024, 1024 + (i + 1) * 1024) for i in range(4)]

        for st in range(4):
            for h in range(NH):
                pb = (h % 2) * 64
                hc = h // 2
                em.guardw("tensor", ("bank", h % 2))
                s2mk = em.op("tensor", lambda E, st=st, pb=pb, hc=hc, h=h:
                             E.matmul(bank(h % 2, 512),
                                      gtT_sb[pb:pb + 64, hc * 128:(hc + 1) * 128],
                                      RB[pb:pb + 64, hc * 2048 + st * 512: hc * 2048 + (st + 1) * 512],
                                      start=True, stop=True), inc="tM")
                mk[("S2", st, h)] = s2mk
                em.wait("scalar", "tM", s2mk)
                em.guardw("scalar", ("f3", h % 2))
                fmk = em.op("scalar", lambda E, h=h:
                            E.activation(F3T[h % 2], bank(h % 2, 512), AF.Exp, bias=c20n, scale=0.125),
                            inc="sM")
                em.consumed(("bank", h % 2), "sM", fmk)
                em.wait("tensor", "sM", fmk)
                em.wait("tensor", "vM", MKGTX)
                for sco in range(4):
                    em.guardw("tensor", ("bank", 2 + sco % 2))
                    gvmk = em.op("tensor", lambda E, h=h, sco=sco:
                                 E.matmul(bank(2 + sco % 2, 65), F3T[h % 2][:, sco * 128:(sco + 1) * 128],
                                          gtx_sb[:, h * 65:(h + 1) * 65], start=True, stop=True),
                                 inc="tM")
                    if sco == 3:
                        em.consumed(("f3", h % 2), "tM", gvmk)
                    em.wait("vector", "tM", gvmk)
                    if h == 0:
                        em.guardw("vector", ("gwin", sco))
                    rlmk = em.op("vector", lambda E, sco=sco:
                                 E.reciprocal(sc(SC_RL), bank(2 + sco % 2, 1, off=64)), inc="vM")
                    em.wait("vector", "vM", rlmk)
                    emk = em.op("vector", lambda E, sco=sco, h=h:
                                E.tensor_scalar(GWIN[sco][:, h * 64:(h + 1) * 64],
                                                bank(2 + sco % 2, 64), sc(SC_RL), None, ALU.mult),
                                inc="vM")
                    em.consumed(("bank", 2 + sco % 2), "vM", emk)
                    mk[("GVe", st, sco)] = emk
            for sco in range(4):
                for fc in range(FCN):
                    em.wait("tensor", "vM", mk[("GVe", st, sco)])
                    em.guardw("tensor", ("bank", 4 + fc % 2))
                    tmk = em.op("tensor", lambda E, sco=sco, fc=fc:
                                E.transpose(bank(4 + fc % 2, 128), GWIN[sco][:, fc * 128:(fc + 1) * 128],
                                            ident[:, :]), inc="tM")
                    em.wait("vector", "tM", tmk)
                    emk = em.op("vector", lambda E, st=st, sco=sco, fc=fc:
                                E.tensor_copy(RB[:, fc * 2048 + st * 512 + sco * 128:
                                                 fc * 2048 + st * 512 + (sco + 1) * 128],
                                              bank(4 + fc % 2, 128)), inc="vM")
                    em.consumed(("bank", 4 + fc % 2), "vM", emk)
                em.consumed(("gwin", sco), "tM", tmk)
        if debug_spills:
            em.wait("sync", "vM", em.cnt["vM"])
            em.op("sync", lambda E: E.dma_start(dbg_glob, RB[:, :]), inc="cPRE", n=16)
        em.flush(nc)

        GWC = rc(0, 4096)
        MIXS = [rc(4096 + i * 2048, 4096 + (i + 1) * 2048) for i in range(2)]
        SSL = [rc(8192 + i * 512, 8192 + (i + 1) * 512) for i in range(2)]
        OWR = rc(0, 8192)

        def catT(i, tt):
            if i < 8:
                return laT(tt, i, 0, 512)
            return RB[:, (i - 8) * 2048 + tt * 512: (i - 8) * 2048 + (tt + 1) * 512]

        gidx = 0
        for jcp in range(4):
            em.guardw("sync", "rc_gw")
            src = gw_d.rearrange("(i p) j -> p i j", p=128)[:, :, jcp * 256:(jcp + 1) * 256]
            gwmk = em.op("sync", lambda E, s=src:
                         E.dma_start(GWC.rearrange("p (i c) -> p i c", c=256), s), inc="cGW", n=16)
            for jl in range(2):
                jc = jcp * 2 + jl
                for tt in range(4):
                    em.wait("tensor", "cGW", gwmk)
                    em.guardw("tensor", ("bank", gidx % 2))
                    for i in range(16):
                        lastmm = i == 15
                        v = em.op("tensor", lambda E, i=i, jl=jl, tt=tt, b=gidx % 2:
                                  E.matmul(bank(b, 512),
                                           GWC[:, i * 256 + jl * 128: i * 256 + (jl + 1) * 128],
                                           catT(i, tt), start=(i == 0), stop=(i == 15)),
                                  inc="tM" if lastmm else None)
                    gtmm = v
                    em.consumed("rc_gw", "tM", gtmm)
                    em.wait("scalar", "tM", gtmm)
                    if tt == 0:
                        em.guardw("scalar", ("mixs", jc % 2))
                    sgmk = em.op("scalar", lambda E, tt=tt, jc=jc, b=gidx % 2:
                                 E.activation(MIXS[jc % 2][:, tt * 512:(tt + 1) * 512], bank(b, 512),
                                              AF.Sigmoid, bias=gateb_sb[:, jc:jc + 1], scale=1.0),
                                 inc="sM")
                    em.consumed(("bank", gidx % 2), "sM", sgmk)
                    gidx += 1
                em.wait("gpsimd", "sM", sgmk)
                mspk = em.op("gpsimd", lambda E, jc=jc:
                             E.dma_start(mix_sp[:, jc * 2048:(jc + 1) * 2048], MIXS[jc % 2]),
                             inc="cMX%d" % (jc % 2), n=16)
                mk[("mxsp", jc)] = mspk
                em.consumed(("mixs", jc % 2), "cMX%d" % (jc % 2), mspk)

        for jc in range(FCN):
            em.guardw("gpsimd", ("mixs", jc % 2))
            em.wait("gpsimd", "cMX%d" % (jc % 2), mk[("mxsp", jc)])
            mrk = em.op("gpsimd", lambda E, jc=jc:
                        E.dma_start(MIXS[jc % 2], mix_sp[:, jc * 2048:(jc + 1) * 2048]),
                        inc="cGA%d" % (jc % 2), n=16)
            em.wait("vector", "cGA%d" % (jc % 2), mrk)
            if jc == 0:
                em.wait("vector", "tM", em.cnt["tM"])
            lT = RA.rearrange("p (w f t) -> p w f t", f=8, t=512)[:, :, jc, :]
            gT = RB[:, jc * 2048:(jc + 1) * 2048].rearrange("p (w t) -> p w t", t=512)
            mx = MIXS[jc % 2].rearrange("p (w t) -> p w t", t=512)
            em.op("vector", lambda E, lT=lT, gT=gT: E.tensor_tensor(lT, lT, gT, ALU.subtract))
            em.op("vector", lambda E, lT=lT, mx=mx: E.tensor_tensor(lT, mx, lT, ALU.mult))
            cmbmk = em.op("vector", lambda E, lT=lT, gT=gT:
                          E.tensor_tensor(lT, lT, gT, ALU.add), inc="vM")
            em.consumed(("mixs", jc % 2), "vM", cmbmk)
            em.consumed("rc_gw2", "vM", cmbmk)
            mk[("cmb", jc)] = cmbmk

        if debug_spills:
            em.wait("sync", "vM", mk[("cmb", 7)])
            em.op("sync", lambda E: E.dma_start(dbg_mixed, RA[:, :]), inc="cPRE", n=16)
        em.guardw("sync", "rc_gw")
        em.guardw("sync", "rc_gw2")
        owmk = em.op("sync", lambda E: E.dma_start(
            OWR.rearrange("p (k c) -> p k c", c=1024),
            ow_d.rearrange("(k p) c -> p k c", p=128)), inc="cGW", n=16)
        sidx = 0
        for tc in range(NCH):
            w = tc // 4
            for ot in range(2):
                em.wait("tensor", "cGW", owmk)
                em.wait("tensor", "vM", mk[("cmb", 7)])
                em.guardw("tensor", ("bank", 2 + sidx % 2))
                for k in range(8):
                    lastmm = k == 7
                    lhs = RA[:, w * 4096 + k * 512 + (tc % 4) * 128: w * 4096 + k * 512 + (tc % 4 + 1) * 128]
                    v = em.op("tensor", lambda E, k=k, ot=ot, lhs=lhs, b=2 + sidx % 2:
                              E.matmul(bank(b, 512), lhs,
                                       OWR[:, k * 1024 + ot * 512: k * 1024 + (ot + 1) * 512],
                                       start=(k == 0), stop=(k == 7)),
                              inc="tM" if lastmm else None)
                em.wait("vector", "tM", v)
                em.guardw("vector", ("ssl", sidx % 2))
                samk = em.op("vector", lambda E, ot=ot, b=2 + sidx % 2, sl=SSL[sidx % 2]:
                             E.tensor_add(sl, bank(b, 512), outb_sb[:, ot * 512:(ot + 1) * 512]),
                             inc="vM")
                em.consumed(("bank", 2 + sidx % 2), "vM", samk)
                em.wait("gpsimd", "vM", samk)
                spmk = em.op("gpsimd", lambda E, tc=tc, ot=ot, sl=SSL[sidx % 2]:
                             E.dma_start(s_sp[tc * 128:(tc + 1) * 128, ot * 512:(ot + 1) * 512], sl),
                             inc="cSS%d" % (sidx % 2), n=16)
                em.consumed(("ssl", sidx % 2), "cSS%d" % (sidx % 2), spmk)
                sidx += 1
        em.flush(nc)
        if bisect == "ph3":
            emit_dummy_out()
            return nc

        GELA = RB[:, 0:8192]
        W1R = [RB[:, 8192 + i * 2048: 8192 + (i + 1) * 2048] for i in range(2)]
        W2R = [RB[:, 12288 + i * 2048: 12288 + (i + 1) * 2048] for i in range(2)]
        GATH = [rc(i * 1024, (i + 1) * 1024) for i in range(2)]
        LBG = rc(2048, 4096)
        HDN = rc(4096, 6144)
        YSC = [rc(6144 + i * 2048, 6144 + (i + 1) * 2048) for i in range(2)]
        BT = [rc(10240 + i * 48, 10288 + i * 48) for i in range(2)]
        IDXT = [rc(10336 + i * 2, 10338 + i * 2).bitcast(I32) for i in range(2)]
        TSC = [rc(10344 + i * 4, 10348 + i * 4) for i in range(2)]

        def gel(fc, col0=0, n=256):
            return GELA[:, fc * 256 + col0: fc * 256 + col0 + n]

        NT = 23
        tile_js = list(range(NT))
        statics = ""
        if bisect is not None and ";static=" in bisect:
            bisect, statics = bisect.split(";static=")
        if bisect is not None and bisect.startswith("ph4t"):
            tile_js = list(range(int(bisect[4:])))
        elif bisect is not None and bisect.startswith("ph4j:"):
            tile_js = [int(x) for x in bisect[5:].split(",")]

        def emit_fc2s(j, fc):
            gp = j * 16 + fc // 2
            em.wait("tensor", "sM", mk[("gelu", j, fc)])
            em.wait("tensor", "cX%d" % ((fc // 2) % 2), mk[("w2", gp)])
            v2 = None
            for tcl in range(2):
                for hn in range(2):
                    b = 2 + 2 * tcl + hn
                    if fc == 0:
                        em.guardw("tensor", ("bank", b))
                    lastm = tcl == 1 and hn == 1
                    v2 = em.op("tensor", lambda E, fc=fc, tcl=tcl, hn=hn, b=b:
                               E.matmul(bank(b, 512), gel(fc, tcl * 128, 128),
                                        W2R[(fc // 2) % 2][:, (fc % 2) * 1024 + hn * 512:
                                                           (fc % 2) * 1024 + (hn + 1) * 512],
                                        start=(fc == 0), stop=(fc == NFC - 1)),
                               inc="tM" if lastm else None)
            mk[("fc2", j, fc)] = v2
            em.consumed(("gel", fc), "tM", v2)
            if fc % 2 == 1:
                em.consumed(("w2r_dma", (fc // 2) % 2), "tM", v2)
            return v2

        def emit_tables(j):
            par = j % 2
            em.guardw("gpsimd", ("tsc", par))
            tscmk = em.op("gpsimd", lambda E, j=j, par=par:
                          E.dma_start(TSC[par].rearrange("p (k c) -> p k c", c=2),
                                      gl2_dram[j * 256:(j + 1) * 256, :]
                                      .rearrange("(k p) c -> p k c", p=128)),
                          inc="cGA%d" % par, n=16)
            mk[("tsc", j)] = tscmk
            em.guardw("gpsimd", ("bt", par))
            if "b" in statics:
                btmk = em.op("gpsimd", lambda E, par=par:
                             E.dma_start(BT[par], btab_d[0:128, :]), inc="cGA%d" % par, n=16)
            else:
                btmk = em.op("gpsimd", lambda E, j=j, par=par:
                             E.indirect_dma_start(BT[par], None, btab_d,
                                                  bass.IndirectOffsetOnAxis(BIDX[:, j:j + 1], 0)),
                             inc="cGA%d" % par, n=16)
            mk[("bt", j)] = btmk
            em.wait("vector", "cGA%d" % par, tscmk)
            em.guardw("vector", ("idxt", par))
            m2_ = em.op("vector", lambda E, par=par:
                        E.tensor_copy(IDXT[par], TSC[par][:, 0:4:2]), inc="vM")
            mk[("idxt", j)] = m2_
            em.wait("gpsimd", "vM", mk[("idxt", j)])
            for k in range(2):
                em.guardw("gpsimd", ("gath", par, k))
                if "l" in statics:
                    gamk = em.op("gpsimd", lambda E, k=k:
                                 E.dma_start(GATH[k], lb_tok_sp[0:128, :]),
                                 inc="cGA%d" % par, n=16)
                else:
                    gamk = em.op("gpsimd", lambda E, k=k, par=par:
                                 E.indirect_dma_start(GATH[k], None, lb_tok_sp,
                                                      bass.IndirectOffsetOnAxis(IDXT[par][:, k:k + 1], 0)),
                                 inc="cGA%d" % par, n=16)
                mk[("ga", j, k)] = gamk
            em.consumed(("idxt", par), "cGA%d" % par, gamk)

        emit_tables(tile_js[0])
        for jidx, j in enumerate(tile_js):
            par = j % 2
            btmk = mk[("bt", j)]
            for k in range(2):
                em.wait("tensor", "cGA%d" % par, mk[("ga", j, k)])
            em.guardw("tensor", "lbgbuf")
            for fc in range(FCN):
                for k in range(2):
                    em.guardw("tensor", ("bank", 6 + (fc * 2 + k) % 2))
                    t1 = em.op("tensor", lambda E, fc=fc, k=k, b=6 + (fc * 2 + k) % 2:
                               E.transpose(bank(b, 128), GATH[k][:, fc * 128:(fc + 1) * 128],
                                           ident[:, :]), inc="tM")
                    em.wait("vector", "tM", t1)
                    e1 = em.op("vector", lambda E, fc=fc, k=k, b=6 + (fc * 2 + k) % 2:
                               E.tensor_copy(LBG[:, fc * 256 + k * 128: fc * 256 + (k + 1) * 128],
                                             bank(b, 128)), inc="vM")
                    em.consumed(("bank", 6 + (fc * 2 + k) % 2), "vM", e1)
                    if fc == FCN - 1 and k == 1:
                        mk[("lbg", j)] = e1
            for k in range(2):
                em.consumed(("gath", par, k), "tM", t1)
            em.wait("vector", "cGA%d" % par, btmk)
            em.guardw("vector", "hdn")
            for fc in range(FCN):
                lastv = fc == FCN - 1
                hv = em.op("vector", lambda E, fc=fc, par=par:
                           E.tensor_scalar(HDN[:, fc * 256:(fc + 1) * 256],
                                           LBG[:, fc * 256:(fc + 1) * 256],
                                           BT[par][:, fc:fc + 1], BT[par][:, 8 + fc:9 + fc],
                                           ALU.mult, ALU.add), inc="vM" if lastv else None)
            MKHDN = hv
            em.consumed("lbgbuf", "vM", MKHDN)
            em.consumed(("bt", par), "vM", MKHDN)
            em.wait("tensor", "vM", MKHDN)
            if jidx >= 1:
                em.wait("tensor", "vM", mk.get(("ysc", tile_js[jidx - 1])))
            for fc in range(NFC):
                if fc == 16 and jidx + 1 < len(tile_js):
                    emit_tables(tile_js[jidx + 1])
                if fc % 2 == 0:
                    pr = fc // 2
                    gp = j * 16 + pr
                    em.guardw("gpsimd", ("w1r", pr % 2))
                    if "w" in statics:
                        mk[("w1", gp)] = em.op("gpsimd", lambda E, pr=pr:
                                               E.dma_start(W1R[pr % 2],
                                                           w1l_d[pr * 128:(pr + 1) * 128, :]),
                                               inc="cW2%d" % (pr % 2), n=16)
                    else:
                        mk[("w1", gp)] = em.op("gpsimd", lambda E, j=j, pr=pr:
                                               E.indirect_dma_start(W1R[pr % 2], None, w1l_d,
                                                                    bass.IndirectOffsetOnAxis(WIDX[:, j:j + 1], 0),
                                                                    element_offset=pr * 262144),
                                               inc="cW2%d" % (pr % 2), n=16)
                    em.guardw("gpsimd", ("w2r_dma", pr % 2))
                    if "w" in statics:
                        mk[("w2", gp)] = em.op("gpsimd", lambda E, pr=pr:
                                               E.dma_start(W2R[pr % 2],
                                                           w2_d[pr * 128:(pr + 1) * 128, :]),
                                               inc="cX%d" % (pr % 2), n=16)
                    else:
                        mk[("w2", gp)] = em.op("gpsimd", lambda E, j=j, pr=pr:
                                               E.indirect_dma_start(W2R[pr % 2], None, w2_d,
                                                                    bass.IndirectOffsetOnAxis(WIDX[:, j:j + 1], 0),
                                                                    element_offset=pr * 262144),
                                               inc="cX%d" % (pr % 2), n=16)
                em.wait("tensor", "cW2%d" % ((fc // 2) % 2), mk[("w1", j * 16 + fc // 2)])
                em.guardw("tensor", ("bank", fc % 2))
                for jj in range(FCN):
                    lastm = jj == FCN - 1
                    fv = em.op("tensor", lambda E, fc=fc, jj=jj:
                               E.matmul(bank(fc % 2, 256),
                                        W1R[(fc // 2) % 2][:, (fc % 2) * 1024 + jj * 128:
                                                           (fc % 2) * 1024 + (jj + 1) * 128],
                                        HDN[:, jj * 256:(jj + 1) * 256],
                                        start=(jj == 0), stop=(jj == FCN - 1)),
                               inc="tM" if lastm else None)
                mk[("fc1", j, fc)] = fv
                if fc % 2 == 1:
                    em.consumed(("w1r", (fc // 2) % 2), "tM", fv)
                em.wait("scalar", "tM", mk[("fc1", j, fc)])
                em.guardw("scalar", ("gel", fc))
                gmk = em.op("scalar", lambda E, fc=fc, par=par:
                            E.activation(gel(fc), bank(fc % 2, 256), AF.Gelu_apprx_tanh,
                                         bias=BT[par][:, 16 + fc: 17 + fc], scale=1.0), inc="sM")
                mk[("gelu", j, fc)] = gmk
                em.consumed(("bank", fc % 2), "sM", gmk)
                if fc >= 1:
                    emit_fc2s(j, fc - 1)
            emit_fc2s(j, NFC - 1)
            em.consumed("hdn", "tM", mk[("fc1", j, NFC - 1)])
            em.consumed(("bt", par), "sM", mk[("gelu", j, NFC - 1)])
            em.wait("vector", "tM", mk[("fc2", j, NFC - 1)])
            em.guardw("vector", ("yscb", par))
            for tcl in range(2):
                for hn in range(2):
                    b = 2 + 2 * tcl + hn
                    yv = em.op("vector", lambda E, tcl=tcl, hn=hn, b=b, par=par:
                               E.tensor_scalar(YSC[par][:, tcl * 1024 + hn * 512:
                                                        tcl * 1024 + (hn + 1) * 512],
                                               bank(b, 512),
                                               TSC[par][:, 1 + 2 * tcl: 2 + 2 * tcl], None,
                                               ALU.mult), inc="vM")
                    em.consumed(("bank", b), "vM", yv)
            mk[("ysc", j)] = yv
            em.consumed(("tsc", par), "vM", yv)
            em.wait("sync", "vM", mk[("ysc", j)])
            for k in range(2):
                ysmk = em.op("sync", lambda E, j=j, k=k, par=par:
                             E.dma_start(y_dram[j * 256 + k * 128: j * 256 + (k + 1) * 128, :],
                                         YSC[par][:, k * 1024:(k + 1) * 1024]),
                             inc="cSP%d" % par, n=16)
            em.consumed(("yscb", par), "cSP%d" % par, ysmk)
        zmk = em.op("vector", lambda E: E.memset(rc(0, 2048), 0.0), inc="vM")
        em.wait("sync", "vM", zmk)
        em.op("sync", lambda E: E.dma_start(
            y_dram[NT * 256:6144, :].rearrange("(p k) c -> p (k c)", p=128), rc(0, 2048)),
            inc="cPRE", n=16)
        em.op("sync", lambda E: E.dma_start(y_dram[6144:6145, :], rc(0, 1024)[0:1, :]),
              inc="cPRE", n=16)
        em.flush(nc)
        if bisect is not None and bisect.startswith("ph4"):
            emit_dummy_out()
            return nc

        G1 = [RB[:, i * 1024: (i + 1) * 1024] for i in range(4)]
        G2 = [RB[:, 4096 + i * 1024: 4096 + (i + 1) * 1024] for i in range(4)]
        XF = [RB[:, 8192 + i * 1024: 8192 + (i + 1) * 1024] for i in range(4)]
        SF = [RB[:, 12288 + i * 1024: 12288 + (i + 1) * 1024] for i in range(4)]
        PD = 3

        def ph5_issue(c):
            q = c % 4
            em.guardw("gpsimd", ("pf", q))
            mk[("xl", c)] = em.op("gpsimd", lambda E, c=c, q=q:
                                  E.dma_start(XF[q], x_d[c * 128:(c + 1) * 128, :]),
                                  inc="cX%d" % (c % 2), n=16)
            mk[("sl", c)] = em.op("gpsimd", lambda E, c=c, q=q:
                                  E.dma_start(SF[q], s_sp[c * 128:(c + 1) * 128, :]),
                                  inc="cSS%d" % (c % 2), n=16)
            em.guardw("gpsimd", ("gb", q))
            em.op("gpsimd", lambda E, c=c, q=q:
                  E.indirect_dma_start(G1[q], None, y_dram,
                                       bass.IndirectOffsetOnAxis(PP1I[:, c:c + 1], 0)),
                  inc="cGA%d" % (c % 2), n=16)
            mk[("g2", c)] = em.op("gpsimd", lambda E, c=c, q=q:
                                  E.indirect_dma_start(G2[q], None, y_dram,
                                                       bass.IndirectOffsetOnAxis(PP2I[:, c:c + 1], 0)),
                                  inc="cGA%d" % (c % 2), n=16)
            for hn in range(2):
                b = (c % 4) * 2 + hn
                em.guardw("tensor", ("bank", b))
                mk[("bc", c, hn)] = em.op("tensor", lambda E, c=c, hn=hn, b=b:
                                          E.matmul(bank(b, 512), combT_sb[:, c * 128:(c + 1) * 128],
                                                   f2b_sb[:, hn * 512:(hn + 1) * 512],
                                                   start=True, stop=True), inc="tM")

        def ph5_compute(c):
            q = c % 4
            em.wait("vector", "cGA%d" % (c % 2), mk[("g2", c)])
            em.op("vector", lambda E, q=q: E.tensor_tensor(G1[q], G1[q], G2[q], ALU.add))
            for hn in range(2):
                b = (c % 4) * 2 + hn
                em.wait("vector", "tM", mk[("bc", c, hn)])
                av = em.op("vector", lambda E, q=q, hn=hn, b=b:
                           E.tensor_tensor(G1[q][:, hn * 512:(hn + 1) * 512],
                                           G1[q][:, hn * 512:(hn + 1) * 512],
                                           bank(b, 512), ALU.add), inc="vM")
                em.consumed(("bank", b), "vM", av)
            em.wait("scalar", "vM", av)
            simk = em.op("scalar", lambda E, q=q:
                         E.activation(G1[q], G1[q], AF.Silu), inc="sM")
            em.wait("vector", "sM", simk)
            em.wait("vector", "cSS%d" % (c % 2), mk[("sl", c)])
            em.wait("vector", "cX%d" % (c % 2), mk[("xl", c)])
            em.op("vector", lambda E, q=q: E.tensor_tensor(SF[q], SF[q], G1[q], ALU.mult))
            fmk = em.op("vector", lambda E, q=q:
                        E.tensor_tensor(SF[q], SF[q], XF[q], ALU.add), inc="vM")
            em.consumed(("gb", q), "vM", fmk)
            em.wait("sync", "vM", fmk)
            omk = em.op("sync", lambda E, c=c, q=q:
                        E.dma_start(out_d[c * 128:(c + 1) * 128, :], SF[q]),
                        inc="cSP%d" % (c % 2), n=16)
            em.consumed(("pf", q), "cSP%d" % (c % 2), omk)

        for c in range(NCH + PD):
            if c >= PD:
                ph5_compute(c - PD)
            if c < NCH:
                ph5_issue(c)
        em.flush(nc)

    return nc


def kernel(**inputs):
    X = np.ascontiguousarray(np.asarray(inputs["L0_attn_x"], dtype=np.float32))
    P = np.ascontiguousarray(np.asarray(inputs["L0_attn_pos_emb"], dtype=np.float32))
    GM = np.asarray(inputs["L0_attn_global_memory"], dtype=np.float32)
    CW = np.asarray(inputs["L0_attn_conv_w"], dtype=np.float32)
    CB = np.asarray(inputs["L0_attn_conv_b"], dtype=np.float32)
    GWm = np.ascontiguousarray(np.asarray(inputs["L0_attn_gate_w"], dtype=np.float32))
    GB = np.asarray(inputs["L0_attn_gate_b"], dtype=np.float32)
    OW = np.ascontiguousarray(np.asarray(inputs["L0_attn_out_w"], dtype=np.float32))
    OB = np.asarray(inputs["L0_attn_out_b"], dtype=np.float32)
    WG = np.ascontiguousarray(np.asarray(inputs["L0_ffn_Wg"], dtype=np.float32))
    AL = np.asarray(inputs["L0_ffn_alpha"], dtype=np.float32)
    LNW = np.asarray(inputs["L0_ffn_ln_w"], dtype=np.float32)
    LNB = np.asarray(inputs["L0_ffn_ln_b"], dtype=np.float32)
    W1 = np.asarray(inputs["L0_ffn_fc1_w"], dtype=np.float32)
    B1 = np.asarray(inputs["L0_ffn_fc1_b"], dtype=np.float32)
    W2 = np.ascontiguousarray(np.asarray(inputs["L0_ffn_fc2_w"], dtype=np.float32))
    B2 = np.ascontiguousarray(np.asarray(inputs["L0_ffn_fc2_b"], dtype=np.float32))

    gmT = np.ascontiguousarray(GM[0].T)
    cwl = np.ascontiguousarray(CW.transpose(1, 2, 0))
    gateb_t = np.ascontiguousarray(GB.reshape(8, 128).T)
    outb_t = np.ascontiguousarray(np.broadcast_to(OB, (128, D)))
    wgv = np.ascontiguousarray(WG.reshape(8, 128, 8).transpose(1, 0, 2).reshape(128, 64))
    alpha_t = np.ascontiguousarray(np.broadcast_to(AL, (128, 8)))
    lnw_t = np.ascontiguousarray(LNW.reshape(8, 8, 128).transpose(2, 0, 1).reshape(128, 64))
    lnb_t = np.ascontiguousarray(LNB.reshape(8, 8, 128).transpose(2, 0, 1).reshape(128, 64))
    w1l = np.ascontiguousarray(
        W1.reshape(8, 8, 128, 32, 128).transpose(0, 3, 2, 1, 4).reshape(8, 32, 128, 1024)
        .reshape(8, 16, 2, 128, 1024).transpose(0, 1, 3, 2, 4).reshape(8 * 16 * 128, 2048))
    w2l = np.ascontiguousarray(
        W2.reshape(8, 16, 2, 128, 1024).transpose(0, 1, 3, 2, 4).reshape(8 * 16 * 128, 2048))
    f1b_t = np.ascontiguousarray(B1.reshape(8, 32, 128).transpose(2, 0, 1).reshape(128, 256))
    convb_t = np.ascontiguousarray(CB.reshape(8, 128).T)
    identm = np.eye(128, dtype=np.float32)
    iot = np.ascontiguousarray(np.broadcast_to(np.arange(2048, dtype=np.float32), (128, 2048)))
    tokt = (np.arange(16, dtype=np.float32)[None, :] * 128
            + np.arange(128, dtype=np.float32)[:, None]).astype(np.float32)
    pio = np.arange(128, dtype=np.float32).reshape(128, 1)
    btab = np.zeros((8, 128, 48), dtype=np.float32)
    lnw_r = LNW.reshape(8, 8, 128)
    lnb_r = LNB.reshape(8, 8, 128)
    f1b_r = B1.reshape(8, 32, 128)
    for e in range(8):
        btab[e, :, 0:8] = lnw_r[e].T
        btab[e, :, 8:16] = lnb_r[e].T
        btab[e, :, 16:48] = f1b_r[e].T
    btab = np.ascontiguousarray(btab.reshape(1024, 48))

    nc = build()
    in_maps = []
    for core in range(8):
        b, j = core // 4, core % 4
        in_maps.append({
            "x": np.ascontiguousarray(X[b, j * TOK:(j + 1) * TOK]),
            "x0": np.ascontiguousarray(X[b, 0:256]),
            "pos": np.ascontiguousarray(P[j * TOK:(j + 1) * TOK]),
            "pos0": np.ascontiguousarray(P[0:256]),
            "gmT": gmT, "cwl": cwl, "gw": GWm, "gateb": gateb_t,
            "ow": OW, "outb": outb_t, "wgv": wgv, "alpha": alpha_t,
            "lnw": lnw_t, "lnb": lnb_t, "w1l": w1l, "w2": w2l,
            "f1b": f1b_t, "f2b": B2, "convb": convb_t, "ident": identm,
            "iot": iot, "tokt": tokt, "pio": pio, "btab": btab,
        })
    res = run_bass_kernel_spmd(nc, in_maps, core_ids=list(range(8)))
    out = np.empty((2, 8192, D), dtype=np.float32)
    for core in range(8):
        b, j = core // 4, core % 4
        out[b, j * TOK:(j + 1) * TOK] = res.results[core]["out"]
    return out
